# version 18
# baseline (speedup 1.0000x reference)
import numpy as np
import concourse.bass as bass
import concourse.mybir as mybir
from concourse.bass_utils import run_bass_kernel_spmd

F32 = mybir.dt.float32
BF16 = mybir.dt.bfloat16
I32 = mybir.dt.int32
ALU = mybir.AluOpType
ACT = mybir.ActivationFunctionType
AX = mybir.AxisListType.X

B, S, D, H, DH, HP = 4, 4096, 1024, 16, 64, 8
NB, TPB, NT = 8, 4, 32
E, FF, NAV = 8, 2048, 128
EPS = 1e-5
BLOCKS = [[0, 7, 2, 5], [1, 6, 3, 4]]

ENGS = ("sync", "tensor", "vector", "scalar", "gpsimd")


class Sem:
    __slots__ = ("h", "n")

    def __init__(self, nc, name):
        self.h = nc.alloc_semaphore(name)
        self.n = 0


class Prog:
    def __init__(self, nc):
        self.nc = nc
        self.ops = {k: [] for k in ENGS}
        self.env = {}
        self.sems = {}
        self.branch_tag = ""


    def sem(self, name):
        if name not in self.sems:
            self.sems[name] = Sem(self.nc, name)
        return self.sems[name]

    def em(self, eng, fn):
        self.ops[eng].append(fn)

    def dma(self, eng, sem, out, in_, slow=False):
        sem.n += 16
        h = sem.h

        def f(e, out=out, in_=in_, h=h, slow=slow):
            e.dma_start(out, in_, allow_slow_non_contiguous=slow).then_inc(h, 16)
        self.em(eng, f)
        return sem.n

    def wait(self, eng, sem, val=None):
        v = sem.n if val is None else val
        self.em(eng, lambda e, h=sem.h, v=v: e.wait_ge(h, v))

    def op(self, eng, fn, inc=None, k=1):
        if inc is not None:
            incs = inc if isinstance(inc, list) else [inc]
            hs = []
            for s0 in incs:
                s0.n += k
                hs.append(s0.h)

            def f(e, fn=fn, hs=hs, kk=k):
                inst = fn(e)
                for h in hs:
                    inst = inst.then_inc(h, kk)
            self.em(eng, f)
            return incs[0].n
        self.em(eng, lambda e, fn=fn: fn(e))
        return None

    def edge(self, eng, fn):
        s = self.sem("seq_" + eng + self.branch_tag)
        self.op(eng, fn, inc=s)
        self.wait(eng, s)


def ap(t, off, dims):
    return bass.AP(t, off, [[t.shape[1], 128]] + dims)


class NS(dict):
    __getattr__ = dict.__getitem__
    __setattr__ = dict.__setitem__


def emit_attention(P, C, role):
    myb = BLOCKS[role]
    perm = myb + BLOCKS[1 - role]
    slot_of = {blk: i for i, blk in enumerate(perm)}
    order = sorted(range(4), key=lambda sl: -myb[sl])
    glob = NS(u=0, g=0, pend=[])
    t, v, sc, sy = "tensor", "vector", "scalar", "sync"

    P.op(v, lambda e: e.memset(C.acc[:, :], 0.0), inc=C.s_mz)

    def emit_mb(slot, qn_ap, base):
        J = myb[slot] if slot is not None else 7
        sig = [slot_of[b] for b in range(J + 1)]
        ch = C.rchain
        P.wait(v, C.s_mb2)
        P.edge(v, lambda e: e.tensor_copy(ch[0:16, 4:5],
                                          C.normk2[:, sig[0]:sig[0] + 1]))
        for b in sig[1:]:
            P.edge(v, lambda e, b=b: e.tensor_max(
                ch[0:16, 4:5], ch[0:16, 4:5], C.normk2[:, b:b + 1]))
        P.op(v, lambda e, qn_ap=qn_ap: e.tensor_mul(
            ch[0:16, 5:6], qn_ap, ch[0:16, 4:5]), inc=C.s_mb1)
        P.wait(sc, C.s_mb1)
        P.op(sc, lambda e: e.activation(ch[0:16, 6:7], ch[0:16, 5:6], ACT.Sqrt),
             inc=C.s_mb2)
        P.wait(t, C.s_mb2)
        P.op(t, lambda e: e.transpose(
            C.Sp[1][0:1, 536:552], ch[0:16, 6:7], C.idf32[:, :]), inc=C.s_mb4)
        P.wait(v, C.s_mb4)
        P.edge(v, lambda e: e.tensor_copy(C.mrow[:, :], C.Sp[1][0:1, 536:552]))
        P.edge(v, lambda e: e.tensor_max(
            C.mpair[:, :], bass.AP(C.mrow, 0, [[16, 1], [2, 8]]),
            bass.AP(C.mrow, 1, [[16, 1], [2, 8]])))
        P.wait(t, P.sem("seq_vector" + P.branch_tag))
        P.wait(t, C.s_mb7)
        P.op(t, lambda e: e.matmul(C.Sp[1][0:128, 528:536], C.ones_f32[0:1, :],
                                   C.mpair[0:1, :], start=True, stop=True),
             inc=C.s_mb6)
        P.wait(v, C.s_mb6)
        P.op(v, lambda e, base=base: e.tensor_scalar_mul(
            C.Mb[:, base:base + 8], C.Sp[1][0:128, 528:536], -1.0),
            inc=C.s_mb7)
        return C.s_mb7.n

    def emit_units(slot, w, qsrc, mb, mbw):
        J = myb[slot] if slot is not None else 7
        P.wait(t, C.s_mb7)
        P.wait(t, C.s_sel)
        for p in range(2):
            if glob.u > 0:
                P.wait(t, C.s_cm)
                P.wait(v, C.s_cm)
                P.wait(v, C.s_ls)
            for g in range(J + 1):
                gi = glob.g
                glob.g += 1
                ksl = slot_of[g]
                if gi >= 2:
                    P.wait(sy, C.s_av, 16 * (gi - 1))
                svt = C.s_vt[gi % 2]
                P.dma(sy, svt,
                      C.kring[:, (gi % 2) * 4096:(gi % 2 + 1) * 4096],
                      C.ksp_h[ksl * 128:(ksl + 1) * 128, :])
                for kt in range(4):
                    ti = ksl * 4 + kt
                    vslot = (gi % 2) * 4 + kt
                    P.dma(sy, svt,
                          C.vring[:, vslot * 1040:(vslot + 1) * 1040],
                          C.vsp_h[ti * 128:(ti + 1) * 128, :])
                vt_tot = svt.n
                for hp in range(4 * p, 4 * p + 4):
                    hpi = hp - 4 * p
                    qo = (hp * 2048 + slot * 512) if slot is not None else hp * 128
                    for kt in range(4):
                        u = glob.u
                        glob.u += 1
                        kcol = (gi % 2) * 4096 + hp * 512 + kt * 128
                        if C.s_ex.n >= 2:
                            P.wait(t, C.s_ex, C.s_ex.n - 1)
                        P.wait(t, svt, vt_tot)
                        for h01 in range(2):
                            bp = 64 * h01
                            P.op(t, lambda e, bp=bp, kcol=kcol, u=u, w=w, qo=qo,
                                 qsrc=qsrc:
                                 e.matmul(C.Sp[u % 2][:, bp * 8:bp * 8 + w],
                                          C.kring[bp:bp + 64, kcol:kcol + 128],
                                          qsrc[bp:bp + 64, qo:qo + w],
                                          start=True, stop=True),
                                 inc=C.s_sc if h01 == 1 else None)
                        P.wait(sc, C.s_sc)
                        if C.s_av.n >= 2:
                            P.wait(sc, C.s_av)
                        P.op(sc, lambda e, u=u, hp=hp, w=w: e.activation(
                            ap(C.A_sb, (u % 4) * 1024, [[512, 2], [1, w]]),
                            ap(C.Sp[u % 2], 0, [[512, 2], [1, w]]),
                            ACT.Exp, bias=C.Mb[:, mb + hp:mb + hp + 1],
                            scale=1.0),
                            inc=C.s_ex)
                        diag = (g == J and slot is not None) or \
                               (slot is None and g == 7 and kt == 3)
                        mkt = kt if slot is not None else 0
                        exwait = C.s_ex.n
                        mkwait = None
                        if diag:
                            P.wait(v, C.s_ex)
                            P.op(v, lambda e, u=u, mkt=mkt, w=w: e.tensor_mul(
                                ap(C.A_sb, (u % 4) * 1024, [[512, 2], [1, w]]),
                                ap(C.A_sb, (u % 4) * 1024, [[512, 2], [1, w]]),
                                ap(C.masks, 384 - 128 * mkt, [[0, 2], [1, w]])),
                                inc=C.s_mk)
                            mkwait = C.s_mk.n

                        def av_emit(u=u, w=w, hp=hp, kt=kt, gi=gi, g=g,
                                    exwait=exwait, mkwait=mkwait, hpi=hpi):
                            if mkwait is not None:
                                P.wait(t, C.s_mk, mkwait)
                            else:
                                P.wait(t, C.s_ex, exwait)
                            if kt == 0 and C.s_acc.n >= 2:
                                P.wait(t, C.s_acc, C.s_acc.n - 1)
                            vb = ((gi % 2) * 4 + kt) * 1040
                            for h01 in range(2):
                                P.op(t, lambda e, hp=hp, h01=h01, vb=vb, u=u,
                                     w=w, kt=kt:
                                     e.matmul(
                                         C.Xp[hp % 2][:, h01 * 512:
                                                      h01 * 512 + w],
                                         C.vring[:, vb + (2 * hp + h01) * 65:
                                                 vb + (2 * hp + h01) * 65 + 65],
                                         C.A_sb[:, (u % 4) * 1024 + h01 * 512:
                                                (u % 4) * 1024 + h01 * 512 + w],
                                         start=(kt == 0), stop=(kt == 3)),
                                     inc=C.s_av if h01 == 1 else None)
                        glob.pend.append(("av", av_emit))
                        while glob.pend and (glob.pend[0][0] != "av" or
                                             sum(1 for k, _ in glob.pend
                                                 if k == "av") > 2):
                            glob.pend.pop(0)[1]()

                    def acc_emit(hpi=hpi, hp=hp, g=g, w=w):
                        P.wait(v, C.s_av)
                        for h01 in range(2):
                            hd = 2 * hpi + h01
                            dst0 = hd * 512
                            if g == 0:
                                P.op(v, lambda e, dst0=dst0, h01=h01, w=w, hp=hp:
                                     e.tensor_copy(
                                         C.acc[:, dst0:dst0 + w],
                                         C.Xp[hp % 2][0:65,
                                                      h01 * 512:h01 * 512 + w]),
                                     inc=C.s_acc if h01 == 1 else None)
                            else:
                                P.op(v, lambda e, dst0=dst0, h01=h01, w=w, hp=hp:
                                     e.tensor_add(
                                         C.acc[:, dst0:dst0 + w],
                                         C.acc[:, dst0:dst0 + w],
                                         C.Xp[hp % 2][0:65,
                                                      h01 * 512:h01 * 512 + w]),
                                     inc=C.s_acc if h01 == 1 else None)
                    glob.pend.append(("acc", acc_emit))
            while glob.pend:
                glob.pend.pop(0)[1]()
            P.wait(sy, C.s_acc)
            P.wait(sy, C.s_mz)
            P.dma(sy, C.s_ls, C.lstack[0:8, 0:512], C.acc[64:65, 0:4096])
            P.wait(v, C.s_ls)
            P.edge(v, lambda e: e.tensor_scalar_add(
                C.lstack[32:40, 0:512], C.lstack[0:8, 0:512], 1e-35))
            P.edge(v, lambda e: e.reciprocal(
                C.lstack[0:8, 0:512], C.lstack[32:40, 0:512]))
            P.op(v, lambda e: e.tensor_copy(
                C.recip_bf[0:8, :], C.lstack[0:8, 0:512]), inc=C.s_rc)
            P.wait(t, C.s_rc)
            P.wait(t, C.s_acc)
            for hd in range(8):
                rc_ap = (C.Sp[1][0:64, 0:512] if hd % 2 == 0
                         else C.Xp[0][0:64, 0:512])
                if hd >= 2:
                    P.wait(t, C.s_cm, C.s_cm.n - 1)
                P.op(t, lambda e, hd=hd, rc_ap=rc_ap: e.matmul(
                    rc_ap, C.onesel[0:8, hd * 64:(hd + 1) * 64],
                    C.recip_bf[0:8, 0:512],
                    start=True, stop=True), inc=C.s_bc)
                P.wait(v, C.s_bc)
                dstt = C.ctxb if p == 0 else C.ctx_lo
                P.op(v, lambda e, dstt=dstt, hd=hd, w=w, rc_ap=rc_ap:
                     e.tensor_mul(
                         dstt[0:64, hd * 512:hd * 512 + w],
                         C.acc[0:64, hd * 512:hd * 512 + w],
                         rc_ap[0:64, 0:w]), inc=C.s_cm)
        P.wait(sy, C.s_cm)
        if w == 512:
            P.dma(sy, C.s_cd, C.ctxb[64:128, 0:4096], C.ctx_lo[0:64, 0:4096])
        else:
            for hd in range(8):
                P.dma(sy, C.s_cd,
                      C.ctxb[64:128, hd * 512:hd * 512 + w],
                      C.ctx_lo[0:64, hd * 512:hd * 512 + w])

    def emit_boundary(slot, w, is_router_tile):
        nqs = 4 if w == 512 else 1
        P.wait(sy, C.s_xm)
        P.wait(sy, C.s_tf)
        a3_base = C.s_a3.n
        xr_tots = {}

        def reload(qs):
            row = (slot * 512 + qs * 128) if w == 512 else (5 * 512 + 384)
            if qs >= 2:
                P.wait(sy, C.s_a3, a3_base + qs - 1)
                P.wait(sy, C.s_xm)
            P.dma(sy, C.s_xr[qs % 2],
                  C.xrel[:, (qs % 2) * 1024:(qs % 2 + 1) * 1024],
                  C.x_h[row:row + 128, :])
            xr_tots[qs] = C.s_xr[qs % 2].n

        reload(0)
        if nqs > 1:
            reload(1)
        P.wait(t, C.s_cd)
        P.wait(t, C.s_wo)
        for qs in range(nqs):
            if qs >= 2:
                P.wait(t, C.s_a3, a3_base + qs - 1)
            SPq = C.Sp[qs % 2]
            for hp in range(8):
                for half in range(2):
                    P.op(t, lambda e, hp=hp, half=half, qs=qs, SPq=SPq: e.matmul(
                        SPq[:, half * 512:(half + 1) * 512],
                        C.ctxb[:, hp * 512 + qs * 128:
                               hp * 512 + qs * 128 + 128],
                        C.woa_sb[:, hp * 1024 + half * 512:
                                 hp * 1024 + (half + 1) * 512],
                        start=(hp == 0), stop=(hp == 7)),
                        inc=C.s_op if (hp == 7 and half == 1) else None)
            P.wait(v, C.s_op)
            P.edge(v, lambda e, SPq=SPq: e.bn_stats(
                C.rstat[:, 0:6], SPq[:, 0:512]))
            P.edge(v, lambda e, SPq=SPq: e.bn_stats(
                C.rstat[:, 6:12], SPq[:, 512:1024]))
            P.edge(v, lambda e: e.bn_aggr(C.rmv[:, 0:2], C.rstat[:, 0:12]))
            P.edge(v, lambda e: e.tensor_scalar_add(
                C.rchain[:, 0:1], C.rmv[:, 1:2], EPS))
            P.op(v, lambda e: e.reciprocal(
                C.rchain[:, 1:2], C.rchain[:, 0:1]), inc=C.s_l3)
            P.wait(sc, C.s_l3)
            P.op(sc, lambda e: e.activation(
                C.rchain[:, 2:3], C.rchain[:, 1:2], ACT.Sqrt), inc=C.s_r3)
            P.wait(v, C.s_r3)
            P.wait(v, C.s_xr[qs % 2], xr_tots[qs])
            P.wait(v, C.s_xm)
            xsl = (qs % 2) * 1024
            xr_ap = C.xrel[:, xsl:xsl + 1024]
            P.edge(v, lambda e: e.tensor_mul(
                C.rchain[:, 3:4], C.rmv[:, 0:1], C.rchain[:, 2:3]))
            P.edge(v, lambda e, xr_ap=xr_ap: e.tensor_scalar_sub(
                xr_ap, xr_ap, C.rchain[:, 3:4]))
            P.op(v, lambda e, xr_ap=xr_ap, SPq=SPq: e.scalar_tensor_tensor(
                xr_ap, SPq[:, :], C.rchain[:, 2:3], xr_ap,
                ALU.mult, ALU.add), inc=C.s_a3)
            P.wait(v, C.s_a3)

            if w == 512:
                P.wait(sy, C.s_a3)
                row = slot * 512 + qs * 128
                P.dma(sy, C.s_xm, C.xm_h[row:row + 128, :], xr_ap)
            if qs + 2 < nqs:
                reload(qs + 2)

    def emit_router(ro):
        P.wait(v, C.s_a3)
        P.edge(v, lambda e: e.bn_stats(C.rstat[:, 0:6], C.xrel[:, ro:ro + 512]))
        P.edge(v, lambda e: e.bn_stats(C.rstat[:, 6:12],
                                       C.xrel[:, ro + 512:ro + 1024]))
        P.edge(v, lambda e: e.bn_aggr(C.rmv[:, 0:2], C.rstat[:, 0:12]))
        P.edge(v, lambda e: e.tensor_scalar_add(
            C.rchain[:, 0:1], C.rmv[:, 1:2], EPS))
        P.op(v, lambda e: e.reciprocal(C.rchain[:, 1:2], C.rchain[:, 0:1]),
             inc=C.s_l3)
        P.wait(sc, C.s_l3)
        P.op(sc, lambda e: e.activation(
            C.rchain[:, 2:3], C.rchain[:, 1:2], ACT.Sqrt), inc=C.s_r3)
        P.wait(v, C.s_r3)
        P.wait(v, C.s_xm)
        P.op(v, lambda e, ro=ro: e.tensor_scalar(
            C.xrel[:, 0:1024], C.xrel[:, ro:ro + 1024],
            C.rmv[:, 0:1], C.rchain[:, 2:3],
            ALU.subtract, ALU.mult), inc=C.s_rt)
        P.wait(sy, C.s_rt)
        P.dma(sy, C.s_tf, C.rtd_h[0:1, :], C.xrel[127:128, 0:1024])
        P.wait(sy, C.s_tf)
        P.dma(sy, C.s_tf, C.t_fm[:, :],
              C.rtd_h.rearrange("o (c p) -> (o p) c", p=128), slow=True)
        P.wait(t, C.s_tf)
        P.wait(t, C.s_wp)
        for fc in range(8):
            P.op(t, lambda e, fc=fc: e.matmul(
                C.Sp[1][0:128, 512:513], C.wp1_sb[:, fc * 128:(fc + 1) * 128],
                C.t_fm[:, fc:fc + 1], start=(fc == 0), stop=(fc == 7)),
                inc=C.s_z1 if fc == 7 else None)
        P.wait(sc, C.s_z1)
        P.op(sc, lambda e: e.activation(
            C.u_sb[:, :], C.Sp[1][0:128, 512:513], ACT.Gelu,
            bias=C.bp1c[:, 0:1], scale=1.0), inc=C.s_u)
        P.wait(t, C.s_u)
        P.op(t, lambda e: e.matmul(
            C.Sp[1][0:8, 516:517], C.wp2_sb[:, :], C.u_sb[:, :],
            start=True, stop=True), inc=C.s_z2)
        P.wait(v, C.s_z2)
        P.op(v, lambda e: e.tensor_add(
            C.lgc[:, :], C.Sp[1][0:8, 516:517], C.bp2c[:, :]), inc=C.s_lg)
        P.wait(t, C.s_lg)
        P.op(t, lambda e: e.transpose(
            C.Sp[1][0:1, 520:528], C.lgc[:, :], C.idf32[0:8, 0:8]), inc=C.s_lt)
        P.wait(v, C.s_lt)
        P.edge(v, lambda e: e.tensor_copy(C.sel_f[:, :], C.Sp[1][0:1, 520:528]))
        P.edge(v, lambda e: e.tensor_reduce(
            C.am3[:, 0:1], C.sel_f[:, :], AX, ALU.max))
        P.edge(v, lambda e: e.tensor_scalar(
            C.am1[:, :], C.sel_f[:, :], C.am3[0:1, 0:1], None, ALU.is_ge))
        P.edge(v, lambda e: e.tensor_scalar(
            C.am1[:, :], C.am1[:, :], 1.0, -1e9, ALU.subtract, ALU.mult))
        P.edge(v, lambda e: e.tensor_add(C.am1[:, :], C.am1[:, :],
                                         C.iota8[:, :]))
        P.edge(v, lambda e: e.tensor_reduce(
            C.am3[:, 1:2], C.am1[:, :], AX, ALU.min))
        P.op(v, lambda e: e.tensor_copy(C.sel_i[:, :], C.am3[:, 1:2]),
             inc=C.s_sel)

    mbw_d = None
    if role == 1:
        mbw_d = emit_mb(None, C.normq2d[:, 0:1], 32)
    mbw_s = [emit_mb(sl, C.normq2[:, sl:sl + 1], oi * 8)
             for oi, sl in enumerate(order)]
    if role == 1:
        emit_units(None, 128, C.qd_sb, 32, mbw_d)
        emit_boundary(None, 128, True)
        emit_router(0)
    for oi, sl in enumerate(order):
        emit_units(sl, 512, C.arena2, oi * 8, mbw_s[oi])
        emit_boundary(sl, 512, role == 0 and oi == 0)
        if role == 0 and oi == 0:
            emit_router(1024)
    for sm in (C.s_vt[0], C.s_vt[1], C.s_xr[0], C.s_xr[1],
               C.s_cd, C.s_ls, C.s_tf, C.s_xm):
        P.wait(sy, sm)
    P.op(v, lambda e: e.memset(C.vscr[0:1, 0:1], 0.0), inc=C.s_attn)
    P.op(sc, lambda e: e.activation(C.vscr[0:1, 1:2], C.ones_f32[0:1, 0:1],
                                    ACT.Copy), inc=C.s_attn)
    P.wait(t, C.s_a3)
    P.wait(t, C.s_cm)
    P.wait(t, C.s_sel)
    P.op(t, lambda e: e.matmul(C.Sp[1][0:1, 552:553], C.ones_f32[0:1, 0:1],
                               C.ones_f32[0:1, 0:1], start=True, stop=True),
         inc=C.s_attn)
    C.s_sdr.n += 16
    P.em(sy, lambda e: e.dma_start(
        C.vscr[0:1, 2:3], C.ones_f32[0:1, 0:1]).then_inc(C.s_sdr.h, 16))


def build(debug=False, phases=("p0", "attn", "ffn")):
    nc = bass.Bass("TRN2", target_bir_lowering=False)
    P = Prog(nc)

    x_h = nc.dram_tensor("x", [S, D], F32, kind="ExternalInput")
    role_h = nc.dram_tensor("role", [1, 1], I32, kind="ExternalInput")
    g_h = nc.dram_tensor("g_ln", [D], F32, kind="ExternalInput")
    wq_h = nc.dram_tensor("Wq", [D, D], F32, kind="ExternalInput")
    wkv_h = nc.dram_tensor("Wkv", [D, 2 * D], F32, kind="ExternalInput")
    woa_h = nc.dram_tensor("Wo_attn", [D, D], F32, kind="ExternalInput")
    wp1_h = nc.dram_tensor("Wp1", [D, NAV], F32, kind="ExternalInput")
    bp1_h = nc.dram_tensor("bp1", [NAV], F32, kind="ExternalInput")
    wp2_h = nc.dram_tensor("Wp2", [NAV, E], F32, kind="ExternalInput")
    bp2_h = nc.dram_tensor("bp2", [E], F32, kind="ExternalInput")
    w1_h = nc.dram_tensor("W1", [E * D, FF], F32, kind="ExternalInput")
    b1_h = nc.dram_tensor("b1", [E, FF], F32, kind="ExternalInput")
    w2_h = nc.dram_tensor("W2", [E * FF, D], F32, kind="ExternalInput")
    b2_h = nc.dram_tensor("b2", [E, D], F32, kind="ExternalInput")
    wof_h = nc.dram_tensor("Wo_ffn", [D, D], F32, kind="ExternalInput")
    bo_h = nc.dram_tensor("bo", [D], F32, kind="ExternalInput")

    y_h = nc.dram_tensor("y", [S // 2, D], F32, kind="ExternalOutput")
    vsp_h = nc.dram_tensor("v_spill", [NT * 128, HP * 2 * 65],
                           BF16, kind="ExternalOutput" if debug else "Internal")
    ksp_h = nc.dram_tensor("k_spill", [NB * 128, HP * 512],
                           BF16, kind="ExternalOutput" if debug else "Internal")
    xm_h = nc.dram_tensor("xmid", [S // 2, D], F32,
                          kind="ExternalOutput" if debug else "Internal")
    rtd_h = nc.dram_tensor("rt_dram", [1, D], F32, kind="Internal")
    if debug:
        dQ_h = nc.dram_tensor("dQ", [128, HP * S // 2], BF16, kind="ExternalOutput")
        dst_h = nc.dram_tensor("dstats", [128, 2 * NT], F32, kind="ExternalOutput")
        dnq_h = nc.dram_tensor("dnq", [16, 4], F32, kind="ExternalOutput")
        dsel_h = nc.dram_tensor("dsel", [1, 1], I32, kind="ExternalOutput")
        dnk_h = nc.dram_tensor("dnk", [16, 8], F32, kind="ExternalOutput")
        dnqd_h = nc.dram_tensor("dnqd", [16, 1], F32, kind="ExternalOutput")

    from contextlib import ExitStack
    stack = ExitStack()
    with stack:
        Esb = lambda *a, **kw: stack.enter_context(nc.sbuf_tensor(*a, **kw))
        Eps = lambda *a, **kw: stack.enter_context(nc.psum_tensor(*a, **kw))

        ident = Esb("ident", [128, 128], BF16)
        idf32 = Esb("idf32", [16, 16], F32)
        ones_bf = Esb("ones_bf", [1, 128], BF16)
        onesblk = Esb("onesblk", [128, HP * 16], BF16)
        g_sb = Esb("g_sb", [128, 8], F32)
        g8q = Esb("g8q", [128, 8], F32)
        normq2 = Esb("normq2", [16, 4], F32)
        normq2d = Esb("normq2d", [16, 1], F32)
        normk2 = Esb("normk2", [16, 8], F32)
        arena1 = Esb("arena1", [128, 8 * 2048], BF16)
        qd_sb = Esb("qd_sb", [128, HP * 128], BF16)
        wp2_sb = Esb("wp2_sb", [128, 8], F32)
        bp1c = Esb("bp1c", [128, 1], F32)
        bp2c = Esb("bp2c", [8, 1], F32)
        b1c = Esb("b1c", [128, 16], F32)
        b2c = Esb("b2c", [128, 8], F32)
        sel_i = Esb("sel_i", [1, 1], I32)
        role_sb = Esb("role_sb", [1, 1], I32)
        gscr = Esb("gscr", [1, 1], F32)
        ones_f32 = Esb("ones_f32", [1, 128], F32)
        lstack = Esb("lstack", [64, 512], F32)
        recip_bf = Esb("recip_bf", [16, 512], BF16)
        Mb = Esb("Mb", [128, 40], F32)
        mrow = Esb("mrow", [1, 16], F32)
        mid = ExitStack()
        stack.enter_context(mid)
        Emid = lambda *a, **kw: mid.enter_context(nc.sbuf_tensor(*a, **kw))
        arena2 = Emid("arena2", [128, HP * S // 2], BF16)

        s = P.sem
        s_gp = s("s_gp")
        s_role = s("s_role")
        s_wl = s("s_wl")
        s_wsc = s("s_wsc")
        s_cl = s("s_cl")
        s_x = s("s_x")
        s_x1 = s("s_x1")
        s_app = s("s_app")
        s_sq = s("s_sq")
        s_ch = s("s_ch")
        s_tp = s("s_tp")
        s_xfe = s("s_xfe")
        s_pp = s("s_pp")
        s_pe = s("s_pe")
        s_nm = s("s_nm")
        s_nr = s("s_nr")
        s_vp = s("s_vp")
        s_ve = s("s_ve")
        s_vs = s("s_vs")
        s_kst = s("s_kst")
        s_ksp = s("s_ksp")
        s_prj = s("s_prj")

        P.op("gpsimd", lambda g: g.memset(ident[:], 1.0))
        P.op("gpsimd", lambda g: g.memset(idf32[:], 1.0))
        P.op("gpsimd", lambda g: g.memset(ones_bf[:], 1.0))
        P.op("gpsimd", lambda g: g.memset(ones_f32[:], 1.0))
        P.edge("gpsimd", lambda g: g.memset(onesblk[:], 0.0))
        P.op("gpsimd", lambda g: g.affine_select(
            ident[:], ident[:], [[1, 128]], ALU.is_equal, 0.0,
            base=0, channel_multiplier=-1))
        P.op("gpsimd", lambda g: g.affine_select(
            idf32[:], idf32[:], [[1, 16]], ALU.is_equal, 0.0,
            base=0, channel_multiplier=-1))
        for hp in range(HP):
            P.op("gpsimd", lambda g, hp=hp: g.memset(
                onesblk[0:64, hp * 16 + 2 * hp: hp * 16 + 2 * hp + 1], 1.0))
            P.op("gpsimd", lambda g, hp=hp: g.memset(
                onesblk[64:128, hp * 16 + 2 * hp + 1: hp * 16 + 2 * hp + 2], 1.0))
        P.op("gpsimd", lambda g: g.memset(gscr[:, :], 0.0), inc=s_gp)
        P.dma("gpsimd", s_role, role_sb[:, :], role_h[:, :])

        P.dma("sync", s_cl, g_sb[:, :], g_h.rearrange("(c p) -> p c", p=128), slow=True)
        P.dma("sync", s_cl, wp2_sb[:, :], wp2_h[:, :])
        P.dma("sync", s_cl, bp1c[:, :], bp1_h.rearrange("(p o) -> p o", o=1))
        P.dma("sync", s_cl, bp2c[:, :], bp2_h.rearrange("(p o) -> p o", o=1))

        for fc in range(8):
            P.dma("gpsimd", s_wl, arena1[:, fc * 2048:(fc + 1) * 2048],
                  wkv_h[fc * 128:(fc + 1) * 128, :])

        if "p0" in phases:
            with ExitStack() as p0s:
                E0 = lambda *a, **kw: p0s.enter_context(nc.sbuf_tensor(*a, **kw))
                xin = E0("xin", [128, 8 * 1024], F32)
                smu = E0("smu", [128, NT], F32)
                ssc = E0("ssc", [128, NT], F32)
                stats = E0("stats", [128, 48], F32)
                mvt = E0("mvt", [128, 8], F32)
                chain = E0("chain", [128, 16], F32)
                xnb = E0("xnb", [128, 2 * 1024], BF16)
                xfb = E0("xfb", [128, 2 * 8 * 512], BF16)
                vstage = E0("vstage", [128, 2 * 1040], BF16)
                kstage = E0("kstage", [128, 2 * 512], BF16)
                sqb = E0("sqb", [128, 3 * 512], BF16)
                wq_sb = E0("wq_sb", [128, 8 * 1024], BF16)
                for fc in range(8):
                    P.dma("gpsimd", s_wl, wq_sb[:, fc * 1024:(fc + 1) * 1024],
                          wq_h[fc * 128:(fc + 1) * 128, :])

                EP0 = lambda *a, **kw: p0s.enter_context(nc.psum_tensor(*a, **kw))
                Tp = [EP0(f"Tp{i}", [128, 1024], BF16) for i in range(2)]
                Cp = [EP0(f"Cp{i}", [128, 512], F32) for i in range(4)]
                Np = [EP0(f"Np{i}", [16, 512], F32) for i in range(2)]

                for i in range(2):
                    P.op("gpsimd", lambda g, i=i: g.memset(
                        ap(vstage, i * 1040 + 64, [[65, 16], [1, 1]]), 1.0),
                        inc=s_gp if i == 1 else None)

                P.wait("vector", s_cl)
                P.edge("vector", lambda v: v.tensor_scalar_mul(
                    g8q[:, :], g_sb[:, :], 0.125))
                P.wait("vector", s_wl)
                for fc in range(8):
                    P.op("vector", lambda v, fc=fc: v.tensor_scalar_mul(
                        wq_sb[:, fc * 1024:(fc + 1) * 1024],
                        wq_sb[:, fc * 1024:(fc + 1) * 1024], g8q[:, fc:fc + 1]))
                    P.op("vector", lambda v, fc=fc: v.tensor_scalar_mul(
                        arena1[:, fc * 2048:(fc + 1) * 2048],
                        arena1[:, fc * 2048:(fc + 1) * 2048], g_sb[:, fc:fc + 1]),
                        inc=s_wsc if fc == 7 else None)

                prj = NS(gsi=0, guard=[None, None, None, None])

                def emit_xloads(tb):
                    for j in range(TPB):
                        ti = 4 * tb + j
                        sl = ti % 8
                        if ti >= 8:
                            P.wait("sync", s_app, ti - 7)
                        P.dma("sync", s_x if tb % 2 == 0 else s_x1,
                              xin[:, sl * 1024:(sl + 1) * 1024],
                              x_h[ti * 128:(ti + 1) * 128, :])

                def emit_ln(tb):
                    P.wait("vector", s_sq, tb)
                    P.wait("vector", s_x if tb % 2 == 0 else s_x1,
                           64 * (tb // 2 + 1))
                    for j in range(TPB):
                        sl = (4 * tb + j) % 8
                        P.op("vector", lambda v, j=j, sl=sl: v.bn_stats(
                            stats[:, 12 * j:12 * j + 6],
                            xin[:, sl * 1024:sl * 1024 + 512]))
                        fn = lambda v, j=j, sl=sl: v.bn_stats(
                            stats[:, 12 * j + 6:12 * j + 12],
                            xin[:, sl * 1024 + 512:sl * 1024 + 1024])
                        if j == 3:
                            P.edge("vector", fn)
                        else:
                            P.op("vector", fn)
                    for j in range(TPB):
                        fn = lambda v, j=j: v.bn_aggr(
                            mvt[:, 2 * j:2 * j + 2], stats[:, 12 * j:12 * j + 12])
                        if j == 3:
                            P.edge("vector", fn)
                        else:
                            P.op("vector", fn)
                    mu_v = bass.AP(mvt, 0, [[8, 128], [2, 4]])
                    var_v = bass.AP(mvt, 1, [[8, 128], [2, 4]])
                    P.op("vector", lambda v, tb=tb: v.tensor_copy(
                        smu[:, 4 * tb:4 * tb + 4], mu_v))
                    P.edge("vector", lambda v, var_v=var_v: v.tensor_scalar_add(
                        chain[:, 0:4], var_v, EPS))
                    P.edge("vector", lambda v: v.reciprocal(
                        chain[:, 4:8], chain[:, 0:4]))
                    P.edge("vector", lambda v, var_v=var_v: v.tensor_mul(
                        chain[:, 8:12], var_v, chain[:, 4:8]))
                    P.edge("vector", lambda v: v.tensor_scalar_add(
                        chain[:, 8:12], chain[:, 8:12], EPS))
                    P.edge("vector", lambda v: v.reciprocal(
                        chain[:, 12:16], chain[:, 8:12]))
                    P.op("vector", lambda v: v.tensor_mul(
                        chain[:, 0:4], chain[:, 4:8], chain[:, 12:16]), inc=s_ch)
                    P.wait("scalar", s_ch)
                    P.op("scalar", lambda sc, tb=tb: sc.activation(
                        ssc[:, 4 * tb:4 * tb + 4], chain[:, 0:4], ACT.Sqrt),
                        inc=s_sq)

                emit_xloads(0)
                emit_xloads(1)
                emit_ln(0)

                for tb in range(NB):
                    if tb + 2 < NB:
                        emit_xloads(tb + 2)

                    P.wait("vector", s_sq)
                    for j in range(TPB):
                        ti = 4 * tb + j
                        sl = ti % 8
                        if ti >= 2:
                            P.wait("vector", s_tp, ti - 1)
                        P.op("vector", lambda v, ti=ti, sl=sl: v.tensor_scalar(
                            xnb[:, (ti % 2) * 1024:(ti % 2 + 1) * 1024],
                            xin[:, sl * 1024:(sl + 1) * 1024],
                            smu[:, ti:ti + 1], ssc[:, ti:ti + 1],
                            ALU.subtract, ALU.mult), inc=s_app)

                    for j in range(TPB):
                        ti = 4 * tb + j
                        if ti == 0:
                            P.wait("tensor", s_gp, 1)
                        P.wait("tensor", s_app, ti + 1)
                        if ti >= 2:
                            P.wait("tensor", s_xfe, ti - 1)
                        for fc in range(8):
                            P.op("tensor", lambda t, ti=ti, fc=fc: t.transpose(
                                Tp[ti % 2][:, fc * 128:(fc + 1) * 128],
                                xnb[:, (ti % 2) * 1024 + fc * 128:
                                    (ti % 2) * 1024 + (fc + 1) * 128],
                                ident[:]), inc=s_tp if fc == 7 else None)

                    for j in range(TPB):
                        ti = 4 * tb + j
                        P.wait("vector", s_tp, ti + 1)
                        if tb >= 2 and j == 0:
                            P.wait("vector", s_vp, 8 * (tb - 1))
                        dst = bass.AP(xfb, (tb % 2) * 4096 + j * 128,
                                      [[8192, 128], [512, 8], [1, 128]])
                        P.op("vector", lambda v, dst=dst, ti=ti: v.tensor_copy(
                            dst, Tp[ti % 2][:, :]), inc=s_xfe)

                    xof = (tb % 2) * 4096
                    P.wait("tensor", s_xfe, 4 * (tb + 1))
                    if tb == 0:
                        P.wait("tensor", s_wsc)
                    kq = [("k", hp) for hp in range(HP)]
                    if tb < 4:
                        kq += [("q", hp) for hp in range(HP)]
                    if tb == 5:
                        kq += [("qd", hp) for hp in range(HP)]
                    base = s_pp.n
                    pend_nm = []
                    for si, (kind, hp) in enumerate(kq):
                        bank = prj.gsi % 4
                        prj.gsi += 1
                        w = 128 if kind == "qd" else 512
                        xq = xof + (384 if kind == "qd" else 0)
                        if prj.guard[bank] is not None:
                            gsem, gval = prj.guard[bank]
                            P.wait("tensor", gsem, gval)
                        if hp == 0:
                            P.wait("tensor", s_nr)
                        for fc in range(8):
                            if kind == "k":
                                lh = arena1[:, fc * 2048 + hp * 128:
                                            fc * 2048 + (hp + 1) * 128]
                            else:
                                lh = wq_sb[:, fc * 1024 + hp * 128:
                                           fc * 1024 + (hp + 1) * 128]
                            P.op("tensor", lambda t, lh=lh, bank=bank, fc=fc, xq=xq, w=w:
                                 t.matmul(Cp[bank][:, 0:w], lh,
                                          xfb[:, xq + fc * 512: xq + fc * 512 + w],
                                          start=(fc == 0), stop=(fc == 7)),
                                 inc=s_pp if fc == 7 else None)
                        if len(pend_nm) >= 2:
                            pend_nm.pop(0)()
                        P.wait("scalar", s_pp)
                        if kind == "k":
                            if tb * 8 + hp >= 2:
                                P.wait("scalar", s_ksp)
                            dst = kstage[:, (hp % 2) * 512:(hp % 2 + 1) * 512]
                        elif kind == "q":
                            dst = arena2[:, hp * 2048 + tb * 512:
                                         hp * 2048 + tb * 512 + 512]
                        else:
                            dst = qd_sb[:, hp * 128:(hp + 1) * 128]
                        P.op("scalar", lambda sc, dst=dst, bank=bank, w=w: sc.activation(
                            dst, Cp[bank][:, 0:w], ACT.Copy),
                            inc=s_kst if kind == "k" else None)
                        if kind == "k":
                            P.wait("gpsimd", s_kst)
                            P.dma("gpsimd", s_ksp,
                                  ksp_h[tb * 128:(tb + 1) * 128,
                                        hp * 512:(hp + 1) * 512],
                                  kstage[:, (hp % 2) * 512:(hp % 2 + 1) * 512])
                        if si >= 2:
                            P.wait("scalar", s_nm, s_nm.n - 1)
                        P.op("scalar", lambda sc, bank=bank, si=si, w=w: sc.activation(
                            sqb[:, (si % 3) * 512:(si % 3) * 512 + w],
                            Cp[bank][:, 0:w], ACT.Square), inc=s_pe)
                        prj.guard[bank] = (s_pe, s_pe.n)
                        petarget = s_pe.n
                        np_i = 1 if kind == "k" else 0
                        last = (hp == 7)

                        def nm_emit(hp=hp, si=si, np_i=np_i, w=w, petarget=petarget,
                                    last=last, kind=kind, tb=tb):
                            P.wait("tensor", s_pe, petarget)
                            P.op("tensor", lambda t: t.matmul(
                                Np[np_i][:, 0:w], onesblk[:, hp * 16: hp * 16 + 16],
                                sqb[:, (si % 3) * 512:(si % 3) * 512 + w],
                                start=(hp == 0), stop=(hp == 7)), inc=s_nm)
                            if last:
                                P.wait("vector", s_nm)
                                if kind == "k":
                                    P.op("vector", lambda v: v.reduce_max(
                                        normk2[:, tb:tb + 1], Np[1][:, :], AX),
                                        inc=s_nr)
                                elif kind == "q":
                                    P.op("vector", lambda v: v.reduce_max(
                                        normq2[:, tb:tb + 1], Np[0][:, :], AX),
                                        inc=s_nr)
                                else:
                                    P.op("vector", lambda v: v.reduce_max(
                                        normq2d[:, 0:1], Np[0][:, 0:128], AX),
                                        inc=s_nr)
                        pend_nm.append(nm_emit)
                    while pend_nm:
                        pend_nm.pop(0)()

                    if tb + 1 < NB:
                        emit_ln(tb + 1)

                    vbase = s_vp.n
                    vebase = s_ve.n
                    for j in range(TPB):
                        ti = 4 * tb + j
                        for half in range(2):
                            vi = j * 2 + half
                            bank = prj.gsi % 4
                            prj.gsi += 1
                            if prj.guard[bank] is not None:
                                gsem, gval = prj.guard[bank]
                                P.wait("tensor", gsem, gval)
                            for fc in range(8):
                                vincs = s_vp if fc == 7 else None
                                P.op("tensor",
                                     lambda t, fc=fc, j=j, half=half, bank=bank, xof=xof:
                                     t.matmul(Cp[bank][:, :],
                                              xfb[:, xof + fc * 512 + j * 128:
                                                  xof + fc * 512 + (j + 1) * 128],
                                              arena1[:, fc * 2048 + 1024 + half * 512:
                                                     fc * 2048 + 1024 + (half + 1) * 512],
                                              start=(fc == 0), stop=(fc == 7)),
                                     inc=vincs)
                            P.wait("scalar", s_vp)
                            if ti >= 2 and half == 0:
                                P.wait("scalar", s_vs)
                            if ti == 0 and half == 0:
                                P.wait("scalar", s_gp)
                            P.op("scalar", lambda sc, j=j, half=half, bank=bank:
                                 sc.activation(
                                     ap(vstage, (j % 2) * 1040 + half * 8 * 65,
                                        [[65, 8], [1, 64]]),
                                     Cp[bank][:, :], ACT.Copy), inc=s_ve)
                            prj.guard[bank] = (s_ve, s_ve.n)
                        P.wait("gpsimd", s_ve)
                        P.dma("gpsimd", s_vs, vsp_h[ti * 128:(ti + 1) * 128, :],
                              vstage[:, (j % 2) * 1040:(j % 2 + 1) * 1040])


                if debug:
                    sd = P.sem("s_dbg")
                    P.wait("sync", s_pe)
                    P.wait("sync", s_nr)
                    P.dma("sync", sd, dQ_h[:, :], arena2[:, :])
                    P.dma("sync", sd, dst_h[:, 0:NT], smu[:, :])
                    P.dma("sync", sd, dst_h[:, NT:2 * NT], ssc[:, :])
                    P.dma("sync", sd, dnq_h[:, :], normq2[:, :])
                    P.dma("sync", sd, dnk_h[:, :], normk2[:, :])
                    P.dma("sync", sd, dnqd_h[:, :], normq2d[:, :])
                    P.wait("sync", s_vs)
                    P.wait("sync", s_ksp)
                    P.wait("sync", s_x)
                    P.wait("sync", s_x1)
                    P.wait("sync", s_cl)
                    P.wait("sync", s_wl)
                    P.wait("sync", s_role)
                    P.wait("sync", sd)

        if "attn" in phases:
            with ExitStack() as ats:
                EA = lambda *a, **kw: ats.enter_context(nc.sbuf_tensor(*a, **kw))
                EPA = lambda *a, **kw: ats.enter_context(nc.psum_tensor(*a, **kw))
                masks = EA("masks", [128, 896], BF16)
                woa_sb = EA("woa_sb", [128, 8 * 1024], BF16)
                wp1_sb = EA("wp1_sb", [128, 8 * 128], F32)
                vring = EA("vring", [128, 8 * 1040], BF16)
                kring = EA("kring", [128, 2 * 4096], BF16)
                A_sb = EA("A_sb", [128, 4 * 1024], BF16)
                acc = EA("acc", [65, 8 * 512], F32)
                ctxb = EA("ctxb", [128, 8 * 512], BF16)
                ctx_lo = EA("ctx_lo", [64, 8 * 512], BF16)
                xrel = EA("xrel", [128, 2 * 1024], F32)
                t_fm = EA("t_fm", [128, 8], F32)
                u_sb = EA("u_sb", [128, 1], F32)
                lgc = EA("lgc", [8, 1], F32)
                sel_f = EA("sel_f", [1, 8], F32)
                am1 = EA("am1", [1, 8], F32)
                am3 = EA("am3", [1, 2], F32)
                iota8 = EA("iota8", [1, 8], F32)
                mpair = EA("mpair", [1, 8], F32)
                onesel = EA("onesel", [8, 8 * 64], BF16)
                rstat = EA("rstat", [128, 12], F32)
                rmv = EA("rmv", [128, 2], F32)
                rchain = EA("rchain", [128, 8], F32)
                vscr = EA("vscr", [1, 4], F32)
                s_sdr = s("s_sdr")
                Sp = [EPA(f"Sp{i}", [128, 1024], F32) for i in range(2)]
                Xp = [EPA(f"Xp{i}", [65, 1024], F32) for i in range(2)]

                s_gp2 = s("s_gp2")
                s_wo = s("s_wo")
                s_wp = s("s_wp")
                s_attn = s("s_attn")
                bsems = {}
                bsems["s_vt"] = [s("s_vt0"), s("s_vt1")]
                bsems["s_xr"] = [s("s_xr0"), s("s_xr1")]
                for nm in ("s_sc s_ex s_mk s_av s_avp s_acc s_gd s_ls "
                           "s_rc s_rr s_bc s_cm s_cd s_op s_l3 s_r3 "
                           "s_xm s_a3 s_rtx s_rt s_tf s_z1 s_u s_z2 s_lg "
                           "s_lt s_sel s_mz s_mb1 s_mb2 s_mb3 s_mb4 s_mb5 "
                           "s_mb6 s_mb7").split():
                    bsems[nm] = s(nm)

                P.wait("gpsimd", s_vp)
                P.wait("gpsimd", s_ve)
                P.wait("gpsimd", s_nm)
                P.wait("gpsimd", s_vs)
                if debug:
                    P.wait("gpsimd", P.sem("s_dbg"))
                    P.wait("sync", P.sem("s_dbg"))
                P.edge("gpsimd", lambda g: g.memset(masks[:], 1.0))
                P.op("gpsimd", lambda g: g.affine_select(
                    masks[:, :], masks[:, :], [[1, 896]], ALU.is_ge, 0.0,
                    base=-384, channel_multiplier=-1))
                P.edge("gpsimd", lambda g: g.memset(onesel[:, :], 1.0))
                for hd in range(8):
                    P.op("gpsimd", lambda g, hd=hd: g.affine_select(
                        onesel[:, hd * 64:(hd + 1) * 64],
                        onesel[:, hd * 64:(hd + 1) * 64],
                        [[0, 64]], ALU.is_equal, 0.0,
                        base=hd, channel_multiplier=-1))
                P.op("gpsimd", lambda g: g.iota(
                    iota8[:, :], [[1, 8]], channel_multiplier=0,
                    allow_small_or_imprecise_dtypes=True), inc=s_gp2)

                for c in range(8):
                    P.dma("gpsimd", s_wo, woa_sb[0:64, c * 1024:(c + 1) * 1024],
                          woa_h[c * 64:(c + 1) * 64, :])
                    P.dma("gpsimd", s_wo, woa_sb[64:128, c * 1024:(c + 1) * 1024],
                          woa_h[(c + 8) * 64:(c + 9) * 64, :])
                P.wait("sync", s_vp)
                P.wait("sync", s_nm)
                for fc in range(8):
                    P.dma("sync", s_wp, wp1_sb[:, fc * 128:(fc + 1) * 128],
                          wp1_h[fc * 128:(fc + 1) * 128, :])

                C = NS(
                    x_h=x_h, xm_h=xm_h, vsp_h=vsp_h, rtd_h=rtd_h,
                    ident=ident, idf32=idf32, ones_bf=ones_bf, masks=masks,
                    normq2=normq2, normq2d=normq2d, normk2=normk2,
                    kring=kring, ksp_h=ksp_h, arena2=arena2, qd_sb=qd_sb,
                    woa_sb=woa_sb, wp1_sb=wp1_sb, wp2_sb=wp2_sb,
                    bp1c=bp1c, bp2c=bp2c, sel_i=sel_i,
                    lstack=lstack, recip_bf=recip_bf, Mb=Mb, mrow=mrow,
                    mpair=mpair, onesel=onesel, ones_f32=ones_f32,
                    vring=vring, A_sb=A_sb, acc=acc, ctxb=ctxb, ctx_lo=ctx_lo,
                    xrel=xrel,
                    t_fm=t_fm, u_sb=u_sb, lgc=lgc, sel_f=sel_f, am1=am1,
                    am3=am3, iota8=iota8, rstat=rstat, rmv=rmv, rchain=rchain,
                    Sp=Sp, Xp=Xp, s_wo=s_wo, s_wp=s_wp, s_sdr=s_sdr,
                    vscr=vscr, s_attn=s_attn,
                    **bsems)

                gate_list = [(s_vp, None), (s_pe, None), (s_nr, None),
                             (s_ve, None), (s_vs, None), (s_ksp, None),
                             (s_app, None),
                             (s_xfe, None), (s_nm, None), (s_sq, None),
                             (s_wsc, None), (s_role, 16), (s_x, None), (s_x1, None),
                             (s_cl, None), (s_wl, None), (s_gp2, 1)]
                for eng in ENGS:
                    for sm, vv in gate_list:
                        P.wait(eng, sm, vv)

                def br_open(engname):
                    def f(e, engname=engname):
                        r = e.alloc_register("rrole_" + engname)
                        e.reg_load(r, role_sb[0:1, 0:1])
                        gd = e.If_eq(r, 0)
                        P.env[("g", engname)] = gd
                        gd.__enter__()
                    return f

                def br_else(engname):
                    def f(e, engname=engname):
                        P.env[("g", engname)].__exit__(None, None, None)
                        ge = e.Else()
                        P.env[("g", engname)] = ge
                        ge.__enter__()
                    return f

                def br_close(engname):
                    def f(e, engname=engname):
                        P.env[("g", engname)].__exit__(None, None, None)
                    return f

                snap = {nm: sm.n for nm, sm in P.sems.items()}
                for eng in ENGS:
                    P.em(eng, br_open(eng))
                P.branch_tag = "A"
                emit_attention(P, C, 0)
                countsA = {nm: sm.n for nm, sm in P.sems.items()}
                for nm, sm in P.sems.items():
                    sm.n = snap.get(nm, 0)
                for eng in ENGS:
                    P.em(eng, br_else(eng))
                P.branch_tag = "B"
                emit_attention(P, C, 1)
                countsB = {nm: sm.n for nm, sm in P.sems.items()}
                P.branch_tag = ""
                for eng in ENGS:
                    P.em(eng, br_close(eng))
                for nm, sm in P.sems.items():
                    if countsA.get(nm, 0) != countsB.get(nm, 0):
                        sm.n = None
                s_attn.n = 3

                if debug:
                    sd2 = P.sem("s_dbg2")
                    P.wait("sync", s_attn, 3)
                    P.wait("sync", P.sems["s_xm"], 16 * 16)
                    P.dma("sync", sd2, dsel_h[:, :], sel_i[:, :])
                    P.wait("sync", sd2)

        if "ffn" in phases:
            mid.close()
            with ExitStack() as ffs:
                EF = lambda *a, **kw: ffs.enter_context(nc.sbuf_tensor(*a, **kw))
                EPF = lambda *a, **kw: ffs.enter_context(nc.psum_tensor(*a, **kw))
                xmr = EF("xmr", [128, 8 * 1024], F32)
                xn2b = EF("xn2b", [128, 4 * 1024], BF16)
                xf2 = EF("xf2", [128, 2 * 8 * 512], BF16)
                h_sb = EF("h_sb", [128, 16 * 512], BF16)
                r_sb = EF("r_sb", [128, 2 * 8 * 512], BF16)
                wof_sb = EF("wof_sb", [128, 8 * 1024], BF16)
                w2_sb = EF("w2_sb", [128, 16 * 1024], BF16)
                bo3b = EF("bo3b", [128, 1024], F32)
                u1 = EF("u1", [128, 2 * 1024], F32)
                fr = EF("fr", [128, 2 * 1024], F32)
                y_sb = EF("y_sb", [128, 2 * 1024], F32)
                fstat = EF("fstat", [128, 12], F32)
                ftab = EF("ftab", [128, 40], F32)
                bo_row = EF("bo_row", [1, D], F32)
                Tp2 = [EPF(f"Tp2{i}", [128, 1024], BF16) for i in range(2)]
                Hp = [EPF(f"Hp{i}", [128, 512], F32) for i in range(2)]
                Rp = [EPF(f"Rp{i}", [128, 512], F32) for i in range(2)]

                s_g1 = s("s_g1")
                s_g2 = s("s_g2")
                s_wf = s("s_wf")
                s_bor = s("s_bor")
                s_xmL = s("s_xmL")
                s_xmL1 = s("s_xmL1")
                s_fst = s("s_fst")
                s_fsq = s("s_fsq")
                s_fap = s("s_fap")
                s_ftp = s("s_ftp")
                s_fxe = s("s_fxe")
                s_hp = s("s_hp")
                s_hg = s("s_hg")
                s_rp = s("s_rp")
                s_re = s("s_re")
                s_wp = s("s_wp2")
                s_fr = s("s_fr")
                s_f5 = s("s_f5")
                s_q5 = s("s_q5")
                s_yv = s("s_yv")
                s_bo3 = s("s_bo3")
                s_nb = s("s_nb")
                s_u1 = s("s_u1")
                s_n5 = s("s_n5")
                s_ap5 = s("s_ap5")
                s_y = s("s_y")

                P.wait("gpsimd", P.sems["s_sel"], 1)
                P.wait("gpsimd", s_wsc)
                P.wait("gpsimd", s_vp)

                def g_gather(g):
                    r = g.alloc_register("selreg")
                    g.reg_load(r, sel_i[0:1, 0:1])
                    P.env["selv"] = g.snap(r, min_val=0, max_val=7)
                P.em("gpsimd", g_gather)

                def gw1(g):
                    sv = P.env["selv"]
                    src = bass.AP(w1_h, sv * (1024 * FF),
                                  [[FF, 128], [128 * FF, 8], [1, FF]])
                    g.dma_start(arena1[:, :].rearrange("p (f c) -> p f c", f=8),
                                src).then_inc(s_g1.h, 16)
                s_g1.n += 16
                P.em("gpsimd", gw1)

                def gb1(g):
                    sv = P.env["selv"]
                    g.dma_start(b1c[:, :],
                                bass.AP(b1_h, sv * FF, [[1, 128], [128, 16]]),
                                allow_slow_non_contiguous=True
                                ).then_inc(s_g1.h, 16)
                s_g1.n += 16
                P.em("gpsimd", gb1)

                def gb2(g):
                    sv = P.env["selv"]
                    g.dma_start(b2c[:, :],
                                bass.AP(b2_h, sv * D, [[1, 128], [128, 8]]),
                                allow_slow_non_contiguous=True
                                ).then_inc(s_g1.h, 16)
                s_g1.n += 16
                P.em("gpsimd", gb2)
                P.wait("gpsimd", P.sems["s_attn"], 3)
                P.wait("gpsimd", P.sems["s_sdr"], 16)

                P.dma("gpsimd", s_bor, bo_row[:, :],
                      bo_h.rearrange("(o d) -> o d", o=1))

                def gw2(g):
                    sv = P.env["selv"]
                    src = bass.AP(w2_h, sv * (FF * D),
                                  [[D, 128], [128 * D, 16], [1, D]])
                    g.dma_start(w2_sb[:, :].rearrange("p (f c) -> p f c", f=16),
                                src).then_inc(s_g2.h, 16)
                s_g2.n += 16
                P.em("gpsimd", gw2)
                for fc in range(8):
                    P.dma("gpsimd", s_wf, wof_sb[:, fc * 1024:(fc + 1) * 1024],
                          wof_h[fc * 128:(fc + 1) * 128, :])

                for eng in ("sync", "tensor", "vector", "scalar"):
                    P.wait(eng, P.sems["s_attn"], 3)
                P.wait("sync", P.sems["s_xm"], 256)

                P.wait("tensor", s_bor)
                for half in range(2):
                    P.op("tensor", lambda e, half=half: e.matmul(
                        Hp[half][:, :], ones_f32[0:1, :],
                        bo_row[0:1, half * 512:(half + 1) * 512],
                        start=True, stop=True), inc=s_wp)
                P.wait("vector", s_wp)
                for half in range(2):
                    P.op("vector", lambda e, half=half: e.tensor_scalar_mul(
                        bo3b[:, half * 512:(half + 1) * 512],
                        Hp[half][:, :], 0.3), inc=s_bo3)

                def xm_loads(tb2):
                    q = "sync" if tb2 % 2 == 0 else "gpsimd"
                    for j in range(4):
                        ti = 4 * tb2 + j
                        if ti >= 8:
                            P.wait(q, s_yv, ti - 8 + 1)
                        P.dma(q, s_xmL if tb2 % 2 == 0 else s_xmL1,
                              xmr[:, (ti % 8) * 1024:(ti % 8 + 1) * 1024],
                              xm_h[ti * 128:(ti + 1) * 128, :])
                xm_loads(0)
                xm_loads(1)

                def emit_ln2(tb2):
                    scol = 20 if tb2 % 2 == 0 else 28
                    P.wait("vector", s_fsq)
                    P.wait("vector", s_xmL if tb2 % 2 == 0 else s_xmL1,
                           64 * (tb2 // 2 + 1))
                    for j in range(4):
                        ti = 4 * tb2 + j
                        xsl = (ti % 8) * 1024
                        P.op("vector", lambda e, xsl=xsl: e.bn_stats(
                            fstat[:, 0:6], xmr[:, xsl:xsl + 512]))
                        P.edge("vector", lambda e, xsl=xsl: e.bn_stats(
                            fstat[:, 6:12], xmr[:, xsl + 512:xsl + 1024]))
                        P.edge("vector", lambda e, ti=ti: e.bn_aggr(
                            ftab[:, (ti % 8) * 2:(ti % 8) * 2 + 2],
                            fstat[:, 0:12]))
                    fvar_v = bass.AP(ftab, ((4 * tb2) % 8) * 2 + 1,
                                     [[40, 128], [2, 4]])
                    P.edge("vector", lambda e, fvar_v=fvar_v: e.tensor_scalar_add(
                        ftab[:, 16:20], fvar_v, EPS))
                    P.op("vector", lambda e: e.reciprocal(
                        ftab[:, 16:20], ftab[:, 16:20]), inc=s_fst)
                    P.wait("scalar", s_fst)
                    P.op("scalar", lambda e, scol=scol: e.activation(
                        ftab[:, scol:scol + 4], ftab[:, 16:20], ACT.Sqrt),
                        inc=s_fsq)

                def emit_applies(tb2):
                    scol = 20 if tb2 % 2 == 0 else 28
                    P.wait("vector", s_fsq)
                    for j in range(4):
                        ti = 4 * tb2 + j
                        xsl = (ti % 8) * 1024
                        if ti >= 4:
                            P.wait("vector", s_ftp, ti - 3)
                        P.op("vector", lambda e, ti=ti, xsl=xsl, j=j, scol=scol:
                             e.tensor_scalar(
                                 xn2b[:, (ti % 4) * 1024:(ti % 4 + 1) * 1024],
                                 xmr[:, xsl:xsl + 1024],
                                 ftab[:, (ti % 8) * 2:(ti % 8) * 2 + 1],
                                 ftab[:, scol + j:scol + j + 1],
                                 ALU.subtract, ALU.mult), inc=s_fap)

                emit_ln2(0)
                emit_applies(0)
                for tb2 in range(4):
                    for j in range(4):
                        ti = 4 * tb2 + j
                        P.wait("tensor", s_fap, ti + 1)
                        if ti >= 2:
                            P.wait("tensor", s_fxe, ti - 1)
                        for fc in range(8):
                            P.op("tensor", lambda e, ti=ti, fc=fc: e.transpose(
                                Tp2[ti % 2][:, fc * 128:(fc + 1) * 128],
                                xn2b[:, (ti % 4) * 1024 + fc * 128:
                                     (ti % 4) * 1024 + (fc + 1) * 128],
                                ident[:]), inc=s_ftp if fc == 7 else None)
                        P.wait("vector", s_ftp, ti + 1)
                        if tb2 >= 2 and j == 0:
                            P.wait("vector", s_hp, 16 * (tb2 - 1))
                        dst = bass.AP(xf2, (tb2 % 2) * 4096 + j * 128,
                                      [[8192, 128], [512, 8], [1, 128]])
                        P.op("vector", lambda e, dst=dst, ti=ti: e.tensor_copy(
                            dst, Tp2[ti % 2][:, :]), inc=s_fxe)
                    P.wait("tensor", s_fxe, 4 * (tb2 + 1))
                    if s_fr.n >= 2:
                        P.wait("tensor", s_fr, s_fr.n - 2)
                    if tb2 == 0:
                        P.wait("tensor", s_g1)
                    hb = s_hp.n
                    for Fs in range(16):
                        bank = Fs % 2
                        if Fs >= 2:
                            P.wait("tensor", s_hg, s_hg.n - 1)
                        for fc in range(8):
                            P.op("tensor", lambda e, Fs=Fs, fc=fc, bank=bank, tb2=tb2:
                                 e.matmul(Hp[bank][:, :],
                                          arena1[:, fc * 2048 + Fs * 128:
                                                 fc * 2048 + (Fs + 1) * 128],
                                          xf2[:, (tb2 % 2) * 4096 + fc * 512:
                                              (tb2 % 2) * 4096 + (fc + 1) * 512],
                                          start=(fc == 0), stop=(fc == 7)),
                                 inc=s_hp if fc == 7 else None)
                        P.wait("scalar", s_hp)
                        if tb2 >= 1 and Fs < 2:
                            P.wait("scalar", s_rp, 8 * tb2)
                        P.op("scalar", lambda e, Fs=Fs, bank=bank: e.activation(
                            h_sb[:, Fs * 512:(Fs + 1) * 512], Hp[bank][:, :],
                            ACT.Gelu, bias=b1c[:, Fs:Fs + 1], scale=1.0),
                            inc=s_hg)
                    if tb2 == 0:
                        P.wait("tensor", s_g2)
                    P.wait("tensor", s_hg)
                    for os in range(8):
                        bank = os % 2
                        if os >= 2:
                            P.wait("tensor", s_re, s_re.n - 1)
                        for fc2 in range(16):
                            P.op("tensor", lambda e, os=os, fc2=fc2, bank=bank:
                                 e.matmul(Rp[bank][:, :],
                                          w2_sb[:, fc2 * 1024 + os * 128:
                                                fc2 * 1024 + (os + 1) * 128],
                                          h_sb[:, fc2 * 512:(fc2 + 1) * 512],
                                          start=(fc2 == 0), stop=(fc2 == 15)),
                                 inc=s_rp if fc2 == 15 else None)
                        P.wait("vector", s_rp)
                        P.op("vector", lambda e, os=os, bank=bank, tb2=tb2:
                             e.tensor_scalar_add(
                                 r_sb[:, (tb2 % 2) * 4096 + os * 512:
                                      (tb2 % 2) * 4096 + (os + 1) * 512],
                                 Rp[bank][:, :], b2c[:, os:os + 1]), inc=s_re)
                    if tb2 + 1 < 4:
                        emit_ln2(tb2 + 1)
                        emit_applies(tb2 + 1)
                    if tb2 == 0:
                        P.wait("tensor", s_wf)
                        P.wait("tensor", s_bo3)
                    for j in range(4):
                        ti = 4 * tb2 + j
                        WB = Hp if j % 2 == 0 else Rp
                        P.wait("tensor", s_re)
                        if ti >= 2:
                            P.wait("tensor", s_fr, s_fr.n - 2)
                        for os in range(8):
                            for half in range(2):
                                P.op("tensor", lambda e, os=os, half=half, j=j,
                                     tb2=tb2, WB=WB:
                                     e.matmul(
                                         WB[half][:, :],
                                         r_sb[:, (tb2 % 2) * 4096 + os * 512 +
                                              j * 128:(tb2 % 2) * 4096 +
                                              os * 512 + j * 128 + 128],
                                         wof_sb[:, os * 1024 + half * 512:
                                                os * 1024 + (half + 1) * 512],
                                         start=(os == 0), stop=(os == 7)),
                                     inc=s_wp if (os == 7 and half == 1) else None)
                        xsl = (ti % 8) * 1024
                        rscol = 20 if tb2 % 2 == 0 else 28
                        P.wait("vector", s_wp)
                        P.edge("vector", lambda e, ti=ti, xsl=xsl, j=j,
                               rscol=rscol: e.tensor_scalar(
                            u1[:, (ti % 2) * 1024:(ti % 2 + 1) * 1024],
                            xmr[:, xsl:xsl + 1024],
                            ftab[:, (ti % 8) * 2:(ti % 8) * 2 + 1],
                            ftab[:, rscol + j:rscol + j + 1],
                            ALU.subtract, ALU.mult))
                        P.wait("vector", s_bo3, 2)
                        P.edge("vector", lambda e, ti=ti: e.tensor_add(
                            u1[:, (ti % 2) * 1024:(ti % 2 + 1) * 1024],
                            u1[:, (ti % 2) * 1024:(ti % 2 + 1) * 1024],
                            bo3b[:, :]))
                        for half in range(2):
                            P.op("vector", lambda e, ti=ti, half=half, WB=WB:
                                 e.scalar_tensor_tensor(
                                     fr[:, (ti % 2) * 1024 + half * 512:
                                        (ti % 2) * 1024 + (half + 1) * 512],
                                     WB[half][:, :], 0.3,
                                     u1[:, (ti % 2) * 1024 + half * 512:
                                        (ti % 2) * 1024 + (half + 1) * 512],
                                     ALU.mult, ALU.add), inc=s_fr)
                        P.wait("vector", s_fr)
                        P.edge("vector", lambda e, ti=ti: e.bn_stats(
                            fstat[:, 0:6],
                            fr[:, (ti % 2) * 1024:(ti % 2) * 1024 + 512]))
                        P.edge("vector", lambda e, ti=ti: e.bn_stats(
                            fstat[:, 6:12],
                            fr[:, (ti % 2) * 1024 + 512:(ti % 2) * 1024 + 1024]))
                        P.edge("vector", lambda e: e.bn_aggr(
                            ftab[:, 24:26], fstat[:, 0:12]))
                        P.edge("vector", lambda e: e.tensor_scalar_add(
                            ftab[:, 26:27], ftab[:, 25:26], EPS))
                        P.edge("vector", lambda e: e.tensor_scalar_mul(
                            ftab[:, 32:33], ftab[:, 24:25], -1.0))
                        P.op("vector", lambda e: e.reciprocal(
                            ftab[:, 27:28], ftab[:, 26:27]), inc=s_f5)
                        P.wait("scalar", s_f5)
                        P.op("scalar", lambda e: e.activation(
                            ftab[:, 26:27], ftab[:, 27:28], ACT.Sqrt), inc=s_q5)
                        P.op("scalar", lambda e, ti=ti: e.activation(
                            y_sb[:, (ti % 2) * 1024:(ti % 2 + 1) * 1024],
                            fr[:, (ti % 2) * 1024:(ti % 2 + 1) * 1024],
                            ACT.Identity, bias=ftab[:, 32:33], scale=1.0),
                            inc=s_ap5)
                        P.wait("vector", s_ap5)
                        P.wait("vector", s_q5)
                        P.wait("vector", s_y)
                        P.op("vector", lambda e, ti=ti, xsl=xsl:
                             e.scalar_tensor_tensor(
                                 y_sb[:, (ti % 2) * 1024:(ti % 2 + 1) * 1024],
                                 y_sb[:, (ti % 2) * 1024:(ti % 2 + 1) * 1024],
                                 ftab[:, 26:27],
                                 xmr[:, xsl:xsl + 1024],
                                 ALU.mult, ALU.add), inc=s_yv)
                        P.wait("sync", s_yv)
                        P.dma("sync", s_y, y_h[ti * 128:(ti + 1) * 128, :],
                              y_sb[:, (ti % 2) * 1024:(ti % 2 + 1) * 1024])
                    if tb2 + 2 <= 3:
                        xm_loads(tb2 + 2)

                P.wait("sync", s_y)
                P.wait("sync", s_xmL)
                P.wait("sync", s_xmL1)
                P.wait("sync", s_g1)
                P.wait("sync", s_g2)
                P.wait("sync", s_wf)
                P.wait("sync", s_bor)
                P.wait("sync", s_role, 16)
                P.wait("sync", s_cl)
                P.wait("sync", s_wl)
                P.wait("sync", s_x)
                P.wait("sync", s_x1)
                P.wait("sync", s_vs)
                P.wait("sync", s_ksp)
                P.wait("sync", P.sems["s_sdr"], 16)

        with nc.Block() as block:
            block.sync(lambda e: [f(e) for f in P.ops["sync"]])
            block.tensor(lambda e: [f(e) for f in P.ops["tensor"]])
            block.vector(lambda e: [f(e) for f in P.ops["vector"]])
            block.scalar(lambda e: [f(e) for f in P.ops["scalar"]])
            block.gpsimd(lambda e: [f(e) for f in P.ops["gpsimd"]])

    return nc



def _perm_for(role):
    return BLOCKS[role] + BLOCKS[1 - role]


def _in_maps(x, args):
    (g_ln, Wq, Wkv, Woa, Wp1, bp1, Wp2, bp2, W1, b1, W2, b2, Wof, bo) = args
    in_maps = []
    for c in range(8):
        b, role = c // 2, c % 2
        perm = _perm_for(role)
        xp = np.concatenate([x[b, blk * 512:(blk + 1) * 512] for blk in perm], axis=0)
        in_maps.append({
            "x": np.ascontiguousarray(xp),
            "role": np.array([[role]], dtype=np.int32),
            "g_ln": np.asarray(g_ln, np.float32),
            "Wq": np.asarray(Wq, np.float32),
            "Wkv": np.asarray(Wkv, np.float32),
            "Wo_attn": np.asarray(Woa, np.float32),
            "Wp1": np.asarray(Wp1, np.float32),
            "bp1": np.asarray(bp1, np.float32),
            "Wp2": np.asarray(Wp2, np.float32),
            "bp2": np.asarray(bp2, np.float32),
            "W1": np.ascontiguousarray(np.asarray(W1, np.float32).reshape(E * D, FF)),
            "b1": np.asarray(b1, np.float32),
            "W2": np.ascontiguousarray(np.asarray(W2, np.float32).reshape(E * FF, D)),
            "b2": np.asarray(b2, np.float32),
            "Wo_ffn": np.asarray(Wof, np.float32),
            "bo": np.asarray(bo, np.float32),
        })
    return in_maps


def kernel(L0_attn_x, L0_attn_g_ln, L0_attn_Wq, L0_attn_Wkv, L0_attn_Wo,
           L0_ffn_Wp1, L0_ffn_bp1, L0_ffn_Wp2, L0_ffn_bp2,
           L0_ffn_W1, L0_ffn_b1, L0_ffn_W2, L0_ffn_b2, L0_ffn_Wo, L0_ffn_bo):
    x = np.ascontiguousarray(np.asarray(L0_attn_x, dtype=np.float32))
    nc = build()
    in_maps = _in_maps(x, (L0_attn_g_ln, L0_attn_Wq, L0_attn_Wkv, L0_attn_Wo,
                           L0_ffn_Wp1, L0_ffn_bp1, L0_ffn_Wp2, L0_ffn_bp2,
                           L0_ffn_W1, L0_ffn_b1, L0_ffn_W2, L0_ffn_b2,
                           L0_ffn_Wo, L0_ffn_bo))
    res = run_bass_kernel_spmd(nc, in_maps, core_ids=list(range(8)))
    y = np.zeros((B, S, D), dtype=np.float32)
    for c in range(8):
        b, role = c // 2, c % 2
        yc = res.results[c]["y"]
        for si, blk in enumerate(BLOCKS[role]):
            y[b, blk * 512:(blk + 1) * 512] = yc[si * 512:(si + 1) * 512]
    return y


# version 19
# speedup vs baseline: 1.0212x; 1.0212x over previous
import numpy as np
import concourse.bass as bass
import concourse.mybir as mybir
from concourse.bass_utils import run_bass_kernel_spmd

F32 = mybir.dt.float32
BF16 = mybir.dt.bfloat16
I32 = mybir.dt.int32
ALU = mybir.AluOpType
ACT = mybir.ActivationFunctionType
AX = mybir.AxisListType.X

B, S, D, H, DH, HP = 4, 4096, 1024, 16, 64, 8
NB, TPB, NT = 8, 4, 32
E, FF, NAV = 8, 2048, 128
EPS = 1e-5
BLOCKS = [[0, 7, 2, 5], [1, 6, 3, 4]]

ENGS = ("sync", "tensor", "vector", "scalar", "gpsimd")


class Sem:
    __slots__ = ("h", "n")

    def __init__(self, nc, name):
        self.h = nc.alloc_semaphore(name)
        self.n = 0


class Prog:
    def __init__(self, nc):
        self.nc = nc
        self.ops = {k: [] for k in ENGS}
        self.env = {}
        self.sems = {}
        self.branch_tag = ""


    def sem(self, name):
        if name not in self.sems:
            self.sems[name] = Sem(self.nc, name)
        return self.sems[name]

    def em(self, eng, fn):
        self.ops[eng].append(fn)

    def dma(self, eng, sem, out, in_, slow=False):
        sem.n += 16
        h = sem.h

        def f(e, out=out, in_=in_, h=h, slow=slow):
            e.dma_start(out, in_, allow_slow_non_contiguous=slow).then_inc(h, 16)
        self.em(eng, f)
        return sem.n

    def wait(self, eng, sem, val=None):
        v = sem.n if val is None else val
        self.em(eng, lambda e, h=sem.h, v=v: e.wait_ge(h, v))

    def op(self, eng, fn, inc=None, k=1):
        if inc is not None:
            incs = inc if isinstance(inc, list) else [inc]
            hs = []
            for s0 in incs:
                s0.n += k
                hs.append(s0.h)

            def f(e, fn=fn, hs=hs, kk=k):
                inst = fn(e)
                for h in hs:
                    inst = inst.then_inc(h, kk)
            self.em(eng, f)
            return incs[0].n
        self.em(eng, lambda e, fn=fn: fn(e))
        return None

    def edge(self, eng, fn):
        s = self.sem("seq_" + eng + self.branch_tag)
        self.op(eng, fn, inc=s)
        self.wait(eng, s)


def ap(t, off, dims):
    return bass.AP(t, off, [[t.shape[1], 128]] + dims)


class NS(dict):
    __getattr__ = dict.__getitem__
    __setattr__ = dict.__setitem__


def emit_attention(P, C, role):
    myb = BLOCKS[role]
    perm = myb + BLOCKS[1 - role]
    slot_of = {blk: i for i, blk in enumerate(perm)}
    order = sorted(range(4), key=lambda sl: -myb[sl])
    glob = NS(u=0, g=0, pend=[])
    t, v, sc, sy = "tensor", "vector", "scalar", "sync"

    P.op(v, lambda e: e.memset(C.acc[:, :], 0.0), inc=C.s_mz)

    def emit_mb(slot, qn_ap, base):
        J = myb[slot] if slot is not None else 7
        sig = [slot_of[b] for b in range(J + 1)]
        ch = C.rchain
        P.wait(v, C.s_mb2)
        P.edge(v, lambda e: e.tensor_copy(ch[0:16, 4:5],
                                          C.normk2[:, sig[0]:sig[0] + 1]))
        for b in sig[1:]:
            P.edge(v, lambda e, b=b: e.tensor_max(
                ch[0:16, 4:5], ch[0:16, 4:5], C.normk2[:, b:b + 1]))
        P.op(v, lambda e, qn_ap=qn_ap: e.tensor_mul(
            ch[0:16, 5:6], qn_ap, ch[0:16, 4:5]), inc=C.s_mb1)
        P.wait(sc, C.s_mb1)
        P.op(sc, lambda e: e.activation(ch[0:16, 6:7], ch[0:16, 5:6], ACT.Sqrt),
             inc=C.s_mb2)
        P.wait(t, C.s_mb2)
        P.op(t, lambda e: e.transpose(
            C.Sp[1][0:1, 536:552], ch[0:16, 6:7], C.idf32[:, :]), inc=C.s_mb4)
        P.wait(v, C.s_mb4)
        P.edge(v, lambda e: e.tensor_copy(C.mrow[:, :], C.Sp[1][0:1, 536:552]))
        P.edge(v, lambda e: e.tensor_max(
            C.mpair[:, :], bass.AP(C.mrow, 0, [[16, 1], [2, 8]]),
            bass.AP(C.mrow, 1, [[16, 1], [2, 8]])))
        P.wait(t, P.sem("seq_vector" + P.branch_tag))
        P.wait(t, C.s_mb7)
        P.op(t, lambda e: e.matmul(C.Sp[1][0:128, 528:536], C.ones_f32[0:1, :],
                                   C.mpair[0:1, :], start=True, stop=True),
             inc=C.s_mb6)
        P.wait(v, C.s_mb6)
        P.op(v, lambda e, base=base: e.tensor_scalar_mul(
            C.Mb[:, base:base + 8], C.Sp[1][0:128, 528:536], -1.0),
            inc=C.s_mb7)
        return C.s_mb7.n

    def emit_units(slot, w, qsrc, mb, mbw):
        J = myb[slot] if slot is not None else 7
        P.wait(t, C.s_mb7)
        P.wait(t, C.s_sel)
        for p in range(2):
            if glob.u > 0:
                P.wait(t, C.s_cm)
                P.wait(v, C.s_cm)
                P.wait(v, C.s_ls)
            for g in range(J + 1):
                gi = glob.g
                glob.g += 1
                ksl = slot_of[g]
                if gi >= 2:
                    P.wait(sy, C.s_av, 16 * (gi - 1))
                svt = C.s_vt[gi % 2]
                P.dma(sy, svt,
                      C.kring[:, (gi % 2) * 4096:(gi % 2 + 1) * 4096],
                      C.ksp_h[ksl * 128:(ksl + 1) * 128, :])
                for kt in range(4):
                    ti = ksl * 4 + kt
                    vslot = (gi % 2) * 4 + kt
                    P.dma(sy, svt,
                          C.vring[:, vslot * 1040:(vslot + 1) * 1040],
                          C.vsp_h[ti * 128:(ti + 1) * 128, :])
                vt_tot = svt.n
                for hp in range(4 * p, 4 * p + 4):
                    hpi = hp - 4 * p
                    qo = (hp * 2048 + slot * 512) if slot is not None else hp * 128
                    for kt in range(4):
                        u = glob.u
                        glob.u += 1
                        kcol = (gi % 2) * 4096 + hp * 512 + kt * 128
                        if C.s_ex.n >= 2:
                            P.wait(t, C.s_ex, C.s_ex.n - 1)
                        P.wait(t, svt, vt_tot)
                        for h01 in range(2):
                            bp = 64 * h01
                            P.op(t, lambda e, bp=bp, kcol=kcol, u=u, w=w, qo=qo,
                                 qsrc=qsrc:
                                 e.matmul(C.Sp[u % 2][:, bp * 8:bp * 8 + w],
                                          C.kring[bp:bp + 64, kcol:kcol + 128],
                                          qsrc[bp:bp + 64, qo:qo + w],
                                          start=True, stop=True),
                                 inc=C.s_sc if h01 == 1 else None)
                        P.wait(sc, C.s_sc)
                        if C.s_av.n >= 2:
                            P.wait(sc, C.s_av)
                        P.op(sc, lambda e, u=u, hp=hp, w=w: e.activation(
                            ap(C.A_sb, (u % 4) * 1024, [[512, 2], [1, w]]),
                            ap(C.Sp[u % 2], 0, [[512, 2], [1, w]]),
                            ACT.Exp, bias=C.Mb[:, mb + hp:mb + hp + 1],
                            scale=1.0),
                            inc=C.s_ex)
                        diag = (g == J and slot is not None) or \
                               (slot is None and g == 7 and kt == 3)
                        mkt = kt if slot is not None else 0
                        exwait = C.s_ex.n
                        mkwait = None
                        if diag:
                            P.wait(v, C.s_ex)
                            P.op(v, lambda e, u=u, mkt=mkt, w=w: e.tensor_mul(
                                ap(C.A_sb, (u % 4) * 1024, [[512, 2], [1, w]]),
                                ap(C.A_sb, (u % 4) * 1024, [[512, 2], [1, w]]),
                                ap(C.masks, 384 - 128 * mkt, [[0, 2], [1, w]])),
                                inc=C.s_mk)
                            mkwait = C.s_mk.n

                        def av_emit(u=u, w=w, hp=hp, kt=kt, gi=gi, g=g,
                                    exwait=exwait, mkwait=mkwait, hpi=hpi):
                            if mkwait is not None:
                                P.wait(t, C.s_mk, mkwait)
                            else:
                                P.wait(t, C.s_ex, exwait)
                            if kt == 0 and C.s_acc.n >= 2:
                                P.wait(t, C.s_acc, C.s_acc.n - 1)
                            vb = ((gi % 2) * 4 + kt) * 1040
                            for h01 in range(2):
                                P.op(t, lambda e, hp=hp, h01=h01, vb=vb, u=u,
                                     w=w, kt=kt:
                                     e.matmul(
                                         C.Xp[hp % 2][:, h01 * 512:
                                                      h01 * 512 + w],
                                         C.vring[:, vb + (2 * hp + h01) * 65:
                                                 vb + (2 * hp + h01) * 65 + 65],
                                         C.A_sb[:, (u % 4) * 1024 + h01 * 512:
                                                (u % 4) * 1024 + h01 * 512 + w],
                                         start=(kt == 0), stop=(kt == 3)),
                                     inc=C.s_av if h01 == 1 else None)
                        glob.pend.append(("av", av_emit))
                        while glob.pend and (glob.pend[0][0] != "av" or
                                             sum(1 for k, _ in glob.pend
                                                 if k == "av") > 2):
                            glob.pend.pop(0)[1]()

                    def acc_emit(hpi=hpi, hp=hp, g=g, w=w):
                        P.wait(v, C.s_av)
                        for h01 in range(2):
                            hd = 2 * hpi + h01
                            dst0 = hd * 512
                            if g == 0:
                                P.op(v, lambda e, dst0=dst0, h01=h01, w=w, hp=hp:
                                     e.tensor_copy(
                                         C.acc[:, dst0:dst0 + w],
                                         C.Xp[hp % 2][0:65,
                                                      h01 * 512:h01 * 512 + w]),
                                     inc=C.s_acc if h01 == 1 else None)
                            else:
                                P.op(v, lambda e, dst0=dst0, h01=h01, w=w, hp=hp:
                                     e.tensor_add(
                                         C.acc[:, dst0:dst0 + w],
                                         C.acc[:, dst0:dst0 + w],
                                         C.Xp[hp % 2][0:65,
                                                      h01 * 512:h01 * 512 + w]),
                                     inc=C.s_acc if h01 == 1 else None)
                    glob.pend.append(("acc", acc_emit))
            while glob.pend:
                glob.pend.pop(0)[1]()
            P.wait(sy, C.s_acc)
            P.wait(sy, C.s_mz)
            P.dma(sy, C.s_ls, C.lstack[0:8, 0:512], C.acc[64:65, 0:4096])
            P.wait(v, C.s_ls)
            P.edge(v, lambda e: e.tensor_scalar_add(
                C.lstack[32:40, 0:512], C.lstack[0:8, 0:512], 1e-35))
            P.edge(v, lambda e: e.reciprocal(
                C.lstack[0:8, 0:512], C.lstack[32:40, 0:512]))
            P.op(v, lambda e: e.tensor_copy(
                C.recip_bf[0:8, :], C.lstack[0:8, 0:512]), inc=C.s_rc)
            P.wait(t, C.s_rc)
            P.wait(t, C.s_acc)
            for hd in range(8):
                rc_ap = (C.Sp[1][0:64, 0:512] if hd % 2 == 0
                         else C.Xp[0][0:64, 0:512])
                if hd >= 2:
                    P.wait(t, C.s_cm, C.s_cm.n - 1)
                P.op(t, lambda e, hd=hd, rc_ap=rc_ap: e.matmul(
                    rc_ap, C.onesel[0:8, hd * 64:(hd + 1) * 64],
                    C.recip_bf[0:8, 0:512],
                    start=True, stop=True), inc=C.s_bc)
                P.wait(v, C.s_bc)
                dstt = C.ctxb if p == 0 else C.ctx_lo
                P.op(v, lambda e, dstt=dstt, hd=hd, w=w, rc_ap=rc_ap:
                     e.tensor_mul(
                         dstt[0:64, hd * 512:hd * 512 + w],
                         C.acc[0:64, hd * 512:hd * 512 + w],
                         rc_ap[0:64, 0:w]), inc=C.s_cm)
        P.wait(sy, C.s_cm)
        if w == 512:
            P.dma(sy, C.s_cd, C.ctxb[64:128, 0:4096], C.ctx_lo[0:64, 0:4096])
        else:
            for hd in range(8):
                P.dma(sy, C.s_cd,
                      C.ctxb[64:128, hd * 512:hd * 512 + w],
                      C.ctx_lo[0:64, hd * 512:hd * 512 + w])

    def emit_boundary(slot, w, is_router_tile):
        nqs = 4 if w == 512 else 1
        P.wait(sy, C.s_xm)
        P.wait(sy, C.s_tf)
        a3_base = C.s_a3.n
        xr_tots = {}

        def reload(qs):
            row = (slot * 512 + qs * 128) if w == 512 else (5 * 512 + 384)
            if qs >= 2:
                P.wait(sy, C.s_a3, a3_base + qs - 1)
                P.wait(sy, C.s_xm)
            P.dma(sy, C.s_xr[qs % 2],
                  C.xrel[:, (qs % 2) * 1024:(qs % 2 + 1) * 1024],
                  C.x_h[row:row + 128, :])
            xr_tots[qs] = C.s_xr[qs % 2].n

        reload(0)
        if nqs > 1:
            reload(1)
        P.wait(t, C.s_cd)
        P.wait(t, C.s_wo)
        for qs in range(nqs):
            if qs >= 2:
                P.wait(t, C.s_a3, a3_base + qs - 1)
            SPq = C.Sp[qs % 2]
            for hp in range(8):
                for half in range(2):
                    P.op(t, lambda e, hp=hp, half=half, qs=qs, SPq=SPq: e.matmul(
                        SPq[:, half * 512:(half + 1) * 512],
                        C.ctxb[:, hp * 512 + qs * 128:
                               hp * 512 + qs * 128 + 128],
                        C.woa_sb[:, hp * 1024 + half * 512:
                                 hp * 1024 + (half + 1) * 512],
                        start=(hp == 0), stop=(hp == 7)),
                        inc=C.s_op if (hp == 7 and half == 1) else None)
            P.wait(v, C.s_op)
            P.edge(v, lambda e, SPq=SPq: e.bn_stats(
                C.rstat[:, 0:6], SPq[:, 0:512]))
            P.edge(v, lambda e, SPq=SPq: e.bn_stats(
                C.rstat[:, 6:12], SPq[:, 512:1024]))
            P.edge(v, lambda e: e.bn_aggr(C.rmv[:, 0:2], C.rstat[:, 0:12]))
            P.edge(v, lambda e: e.tensor_scalar_add(
                C.rchain[:, 0:1], C.rmv[:, 1:2], EPS))
            P.op(v, lambda e: e.reciprocal(
                C.rchain[:, 1:2], C.rchain[:, 0:1]), inc=C.s_l3)
            P.wait(sc, C.s_l3)
            P.op(sc, lambda e: e.activation(
                C.rchain[:, 2:3], C.rchain[:, 1:2], ACT.Sqrt), inc=C.s_r3)
            P.wait(v, C.s_r3)
            P.wait(v, C.s_xr[qs % 2], xr_tots[qs])
            P.wait(v, C.s_xm)
            xsl = (qs % 2) * 1024
            xr_ap = C.xrel[:, xsl:xsl + 1024]
            P.edge(v, lambda e: e.tensor_mul(
                C.rchain[:, 3:4], C.rmv[:, 0:1], C.rchain[:, 2:3]))
            P.edge(v, lambda e, xr_ap=xr_ap: e.tensor_scalar_sub(
                xr_ap, xr_ap, C.rchain[:, 3:4]))
            P.op(v, lambda e, xr_ap=xr_ap, SPq=SPq: e.scalar_tensor_tensor(
                xr_ap, SPq[:, :], C.rchain[:, 2:3], xr_ap,
                ALU.mult, ALU.add), inc=C.s_a3)
            P.wait(v, C.s_a3)

            if w == 512:
                P.wait(sy, C.s_a3)
                row = slot * 512 + qs * 128
                P.dma(sy, C.s_xm, C.xm_h[row:row + 128, :], xr_ap)
            if qs + 2 < nqs:
                reload(qs + 2)

    def emit_router(ro):
        P.wait(v, C.s_a3)
        P.edge(v, lambda e: e.bn_stats(C.rstat[:, 0:6], C.xrel[:, ro:ro + 512]))
        P.edge(v, lambda e: e.bn_stats(C.rstat[:, 6:12],
                                       C.xrel[:, ro + 512:ro + 1024]))
        P.edge(v, lambda e: e.bn_aggr(C.rmv[:, 0:2], C.rstat[:, 0:12]))
        P.edge(v, lambda e: e.tensor_scalar_add(
            C.rchain[:, 0:1], C.rmv[:, 1:2], EPS))
        P.op(v, lambda e: e.reciprocal(C.rchain[:, 1:2], C.rchain[:, 0:1]),
             inc=C.s_l3)
        P.wait(sc, C.s_l3)
        P.op(sc, lambda e: e.activation(
            C.rchain[:, 2:3], C.rchain[:, 1:2], ACT.Sqrt), inc=C.s_r3)
        P.wait(v, C.s_r3)
        P.wait(v, C.s_xm)
        P.op(v, lambda e, ro=ro: e.tensor_scalar(
            C.xrel[:, 0:1024], C.xrel[:, ro:ro + 1024],
            C.rmv[:, 0:1], C.rchain[:, 2:3],
            ALU.subtract, ALU.mult), inc=C.s_rt)
        P.wait(sy, C.s_rt)
        P.dma(sy, C.s_tf, C.rtd_h[0:1, :], C.xrel[127:128, 0:1024])
        P.wait(sy, C.s_tf)
        P.dma(sy, C.s_tf, C.t_fm[:, :],
              C.rtd_h.rearrange("o (c p) -> (o p) c", p=128), slow=True)
        P.wait(t, C.s_tf)
        P.wait(t, C.s_wp)
        for fc in range(8):
            P.op(t, lambda e, fc=fc: e.matmul(
                C.Sp[1][0:128, 512:513], C.wp1_sb[:, fc * 128:(fc + 1) * 128],
                C.t_fm[:, fc:fc + 1], start=(fc == 0), stop=(fc == 7)),
                inc=C.s_z1 if fc == 7 else None)
        P.wait(sc, C.s_z1)
        P.op(sc, lambda e: e.activation(
            C.u_sb[:, :], C.Sp[1][0:128, 512:513], ACT.Gelu,
            bias=C.bp1c[:, 0:1], scale=1.0), inc=C.s_u)
        P.wait(t, C.s_u)
        P.op(t, lambda e: e.matmul(
            C.Sp[1][0:8, 516:517], C.wp2_sb[:, :], C.u_sb[:, :],
            start=True, stop=True), inc=C.s_z2)
        P.wait(v, C.s_z2)
        P.op(v, lambda e: e.tensor_add(
            C.lgc[:, :], C.Sp[1][0:8, 516:517], C.bp2c[:, :]), inc=C.s_lg)
        P.wait(t, C.s_lg)
        P.op(t, lambda e: e.transpose(
            C.Sp[1][0:1, 520:528], C.lgc[:, :], C.idf32[0:8, 0:8]), inc=C.s_lt)
        P.wait(v, C.s_lt)
        P.edge(v, lambda e: e.tensor_copy(C.sel_f[:, :], C.Sp[1][0:1, 520:528]))
        P.edge(v, lambda e: e.tensor_reduce(
            C.am3[:, 0:1], C.sel_f[:, :], AX, ALU.max))
        P.edge(v, lambda e: e.tensor_scalar(
            C.am1[:, :], C.sel_f[:, :], C.am3[0:1, 0:1], None, ALU.is_ge))
        P.edge(v, lambda e: e.tensor_scalar(
            C.am1[:, :], C.am1[:, :], 1.0, -1e9, ALU.subtract, ALU.mult))
        P.edge(v, lambda e: e.tensor_add(C.am1[:, :], C.am1[:, :],
                                         C.iota8[:, :]))
        P.edge(v, lambda e: e.tensor_reduce(
            C.am3[:, 1:2], C.am1[:, :], AX, ALU.min))
        P.op(v, lambda e: e.tensor_copy(C.sel_i[:, :], C.am3[:, 1:2]),
             inc=C.s_sel)

    mbw_d = None
    if role == 1:
        mbw_d = emit_mb(None, C.normq2d[:, 0:1], 32)
    mbw_s = [emit_mb(sl, C.normq2[:, sl:sl + 1], oi * 8)
             for oi, sl in enumerate(order)]
    if role == 1:
        emit_units(None, 128, C.qd_sb, 32, mbw_d)
        emit_boundary(None, 128, True)
        emit_router(0)
    for oi, sl in enumerate(order):
        emit_units(sl, 512, C.arena2, oi * 8, mbw_s[oi])
        emit_boundary(sl, 512, role == 0 and oi == 0)
        if role == 0 and oi == 0:
            emit_router(1024)
    for sm in (C.s_vt[0], C.s_vt[1], C.s_xr[0], C.s_xr[1],
               C.s_cd, C.s_ls, C.s_tf, C.s_xm):
        P.wait(sy, sm)
    P.op(v, lambda e: e.memset(C.vscr[0:1, 0:1], 0.0), inc=C.s_attn)
    P.op(sc, lambda e: e.activation(C.vscr[0:1, 1:2], C.ones_f32[0:1, 0:1],
                                    ACT.Copy), inc=C.s_attn)
    P.wait(t, C.s_a3)
    P.wait(t, C.s_cm)
    P.wait(t, C.s_sel)
    P.op(t, lambda e: e.matmul(C.Sp[1][0:1, 552:553], C.ones_f32[0:1, 0:1],
                               C.ones_f32[0:1, 0:1], start=True, stop=True),
         inc=C.s_attn)
    C.s_sdr.n += 16
    P.em(sy, lambda e: e.dma_start(
        C.vscr[0:1, 2:3], C.ones_f32[0:1, 0:1]).then_inc(C.s_sdr.h, 16))


def build(debug=False, phases=("p0", "attn", "ffn")):
    nc = bass.Bass("TRN2", target_bir_lowering=False)
    P = Prog(nc)

    x_h = nc.dram_tensor("x", [S, D], F32, kind="ExternalInput")
    role_h = nc.dram_tensor("role", [1, 1], I32, kind="ExternalInput")
    g_h = nc.dram_tensor("g_ln", [D], F32, kind="ExternalInput")
    wq_h = nc.dram_tensor("Wq", [D, D], F32, kind="ExternalInput")
    wkv_h = nc.dram_tensor("Wkv", [D, 2 * D], F32, kind="ExternalInput")
    woa_h = nc.dram_tensor("Wo_attn", [D, D], F32, kind="ExternalInput")
    wp1_h = nc.dram_tensor("Wp1", [D, NAV], F32, kind="ExternalInput")
    bp1_h = nc.dram_tensor("bp1", [NAV], F32, kind="ExternalInput")
    wp2_h = nc.dram_tensor("Wp2", [NAV, E], F32, kind="ExternalInput")
    bp2_h = nc.dram_tensor("bp2", [E], F32, kind="ExternalInput")
    w1_h = nc.dram_tensor("W1", [E * D, FF], F32, kind="ExternalInput")
    b1_h = nc.dram_tensor("b1", [E, FF], F32, kind="ExternalInput")
    w2_h = nc.dram_tensor("W2", [E * FF, D], F32, kind="ExternalInput")
    b2_h = nc.dram_tensor("b2", [E, D], F32, kind="ExternalInput")
    wof_h = nc.dram_tensor("Wo_ffn", [D, D], F32, kind="ExternalInput")
    bo_h = nc.dram_tensor("bo", [D], F32, kind="ExternalInput")

    y_h = nc.dram_tensor("y", [S // 2, D], F32, kind="ExternalOutput")
    vsp_h = nc.dram_tensor("v_spill", [NT * 128, HP * 2 * 65],
                           BF16, kind="ExternalOutput" if debug else "Internal")
    ksp_h = nc.dram_tensor("k_spill", [NB * 128, HP * 512],
                           BF16, kind="ExternalOutput" if debug else "Internal")
    xm_h = nc.dram_tensor("xmid", [S // 2, D], F32,
                          kind="ExternalOutput" if debug else "Internal")
    rtd_h = nc.dram_tensor("rt_dram", [1, D], F32, kind="Internal")
    if debug:
        dQ_h = nc.dram_tensor("dQ", [128, HP * S // 2], BF16, kind="ExternalOutput")
        dst_h = nc.dram_tensor("dstats", [128, 2 * NT], F32, kind="ExternalOutput")
        dnq_h = nc.dram_tensor("dnq", [16, 4], F32, kind="ExternalOutput")
        dsel_h = nc.dram_tensor("dsel", [1, 1], I32, kind="ExternalOutput")
        dnk_h = nc.dram_tensor("dnk", [16, 8], F32, kind="ExternalOutput")
        dnqd_h = nc.dram_tensor("dnqd", [16, 1], F32, kind="ExternalOutput")

    from contextlib import ExitStack
    stack = ExitStack()
    with stack:
        Esb = lambda *a, **kw: stack.enter_context(nc.sbuf_tensor(*a, **kw))
        Eps = lambda *a, **kw: stack.enter_context(nc.psum_tensor(*a, **kw))

        ident = Esb("ident", [128, 128], BF16)
        idf32 = Esb("idf32", [16, 16], F32)
        ones_bf = Esb("ones_bf", [1, 128], BF16)
        onesblk = Esb("onesblk", [128, HP * 16], BF16)
        g_sb = Esb("g_sb", [128, 8], F32)
        g8q = Esb("g8q", [128, 8], F32)
        normq2 = Esb("normq2", [16, 4], F32)
        normq2d = Esb("normq2d", [16, 1], F32)
        normk2 = Esb("normk2", [16, 8], F32)
        arena1 = Esb("arena1", [128, 8 * 2048], BF16)
        qd_sb = Esb("qd_sb", [128, HP * 128], BF16)
        wp2_sb = Esb("wp2_sb", [128, 8], F32)
        bp1c = Esb("bp1c", [128, 1], F32)
        bp2c = Esb("bp2c", [8, 1], F32)
        b1c = Esb("b1c", [128, 16], F32)
        b2c = Esb("b2c", [128, 8], F32)
        sel_i = Esb("sel_i", [1, 1], I32)
        role_sb = Esb("role_sb", [1, 1], I32)
        gscr = Esb("gscr", [1, 1], F32)
        ones_f32 = Esb("ones_f32", [1, 128], F32)
        lstack = Esb("lstack", [64, 512], F32)
        recip_bf = Esb("recip_bf", [16, 512], BF16)
        Mb = Esb("Mb", [128, 40], F32)
        mrow = Esb("mrow", [1, 16], F32)
        mid = ExitStack()
        stack.enter_context(mid)
        Emid = lambda *a, **kw: mid.enter_context(nc.sbuf_tensor(*a, **kw))
        arena2 = Emid("arena2", [128, HP * S // 2], BF16)

        s = P.sem
        s_gp = s("s_gp")
        s_role = s("s_role")
        s_wl = s("s_wl")
        s_wsc = s("s_wsc")
        s_cl = s("s_cl")
        s_x = s("s_x")
        s_x1 = s("s_x1")
        s_app = s("s_app")
        s_sq = s("s_sq")
        s_ch = s("s_ch")
        s_tp = s("s_tp")
        s_xfe = s("s_xfe")
        s_pp = s("s_pp")
        s_pe = s("s_pe")
        s_nm = s("s_nm")
        s_nr = s("s_nr")
        s_vp = s("s_vp")
        s_ve = s("s_ve")
        s_vs = s("s_vs")
        s_kst = s("s_kst")
        s_ksp = s("s_ksp")
        s_prj = s("s_prj")

        P.op("gpsimd", lambda g: g.memset(ident[:], 1.0))
        P.op("gpsimd", lambda g: g.memset(idf32[:], 1.0))
        P.op("gpsimd", lambda g: g.memset(ones_bf[:], 1.0))
        P.op("gpsimd", lambda g: g.memset(ones_f32[:], 1.0))
        P.edge("gpsimd", lambda g: g.memset(onesblk[:], 0.0))
        P.op("gpsimd", lambda g: g.affine_select(
            ident[:], ident[:], [[1, 128]], ALU.is_equal, 0.0,
            base=0, channel_multiplier=-1))
        P.op("gpsimd", lambda g: g.affine_select(
            idf32[:], idf32[:], [[1, 16]], ALU.is_equal, 0.0,
            base=0, channel_multiplier=-1))
        for hp in range(HP):
            P.op("gpsimd", lambda g, hp=hp: g.memset(
                onesblk[0:64, hp * 16 + 2 * hp: hp * 16 + 2 * hp + 1], 1.0))
            P.op("gpsimd", lambda g, hp=hp: g.memset(
                onesblk[64:128, hp * 16 + 2 * hp + 1: hp * 16 + 2 * hp + 2], 1.0))
        P.op("gpsimd", lambda g: g.memset(gscr[:, :], 0.0), inc=s_gp)
        P.dma("gpsimd", s_role, role_sb[:, :], role_h[:, :])

        P.dma("sync", s_cl, g_sb[:, :], g_h.rearrange("(c p) -> p c", p=128), slow=True)
        P.dma("sync", s_cl, wp2_sb[:, :], wp2_h[:, :])
        P.dma("sync", s_cl, bp1c[:, :], bp1_h.rearrange("(p o) -> p o", o=1))
        P.dma("sync", s_cl, bp2c[:, :], bp2_h.rearrange("(p o) -> p o", o=1))

        for fc in range(8):
            P.dma("gpsimd", s_wl, arena1[:, fc * 2048:(fc + 1) * 2048],
                  wkv_h[fc * 128:(fc + 1) * 128, :])

        if "p0" in phases:
            with ExitStack() as p0s:
                E0 = lambda *a, **kw: p0s.enter_context(nc.sbuf_tensor(*a, **kw))
                xin = E0("xin", [128, 8 * 1024], F32)
                smu = E0("smu", [128, NT], F32)
                ssc = E0("ssc", [128, NT], F32)
                stats = E0("stats", [128, 48], F32)
                mvt = E0("mvt", [128, 8], F32)
                chain = E0("chain", [128, 16], F32)
                xnb = E0("xnb", [128, 2 * 1024], BF16)
                xfb = E0("xfb", [128, 2 * 8 * 512], BF16)
                vstage = E0("vstage", [128, 2 * 1040], BF16)
                kstage = E0("kstage", [128, 2 * 512], BF16)
                sqb = E0("sqb", [128, 3 * 512], BF16)
                wq_sb = E0("wq_sb", [128, 8 * 1024], BF16)
                for fc in range(8):
                    P.dma("gpsimd", s_wl, wq_sb[:, fc * 1024:(fc + 1) * 1024],
                          wq_h[fc * 128:(fc + 1) * 128, :])

                EP0 = lambda *a, **kw: p0s.enter_context(nc.psum_tensor(*a, **kw))
                Tp = [EP0(f"Tp{i}", [128, 1024], BF16) for i in range(2)]
                Cp = [EP0(f"Cp{i}", [128, 512], F32) for i in range(4)]
                Np = [EP0(f"Np{i}", [16, 512], F32) for i in range(2)]

                for i in range(2):
                    P.op("gpsimd", lambda g, i=i: g.memset(
                        ap(vstage, i * 1040 + 64, [[65, 16], [1, 1]]), 1.0),
                        inc=s_gp if i == 1 else None)

                P.wait("vector", s_cl)
                P.edge("vector", lambda v: v.tensor_scalar_mul(
                    g8q[:, :], g_sb[:, :], 0.125))
                P.wait("vector", s_wl)
                for fc in range(8):
                    P.op("vector", lambda v, fc=fc: v.tensor_scalar_mul(
                        wq_sb[:, fc * 1024:(fc + 1) * 1024],
                        wq_sb[:, fc * 1024:(fc + 1) * 1024], g8q[:, fc:fc + 1]))
                    P.op("vector", lambda v, fc=fc: v.tensor_scalar_mul(
                        arena1[:, fc * 2048:(fc + 1) * 2048],
                        arena1[:, fc * 2048:(fc + 1) * 2048], g_sb[:, fc:fc + 1]),
                        inc=s_wsc if fc == 7 else None)

                prj = NS(gsi=0, guard=[None, None, None, None])

                def emit_xloads(tb):
                    for j in range(TPB):
                        ti = 4 * tb + j
                        sl = ti % 8
                        if ti >= 8:
                            P.wait("sync", s_app, ti - 7)
                        P.dma("sync", s_x if tb % 2 == 0 else s_x1,
                              xin[:, sl * 1024:(sl + 1) * 1024],
                              x_h[ti * 128:(ti + 1) * 128, :])

                def emit_ln(tb):
                    P.wait("vector", s_sq, tb)
                    P.wait("vector", s_x if tb % 2 == 0 else s_x1,
                           64 * (tb // 2 + 1))
                    for j in range(TPB):
                        sl = (4 * tb + j) % 8
                        P.op("vector", lambda v, j=j, sl=sl: v.bn_stats(
                            stats[:, 12 * j:12 * j + 6],
                            xin[:, sl * 1024:sl * 1024 + 512]))
                        fn = lambda v, j=j, sl=sl: v.bn_stats(
                            stats[:, 12 * j + 6:12 * j + 12],
                            xin[:, sl * 1024 + 512:sl * 1024 + 1024])
                        if j == 3:
                            P.edge("vector", fn)
                        else:
                            P.op("vector", fn)
                    for j in range(TPB):
                        fn = lambda v, j=j: v.bn_aggr(
                            mvt[:, 2 * j:2 * j + 2], stats[:, 12 * j:12 * j + 12])
                        if j == 3:
                            P.edge("vector", fn)
                        else:
                            P.op("vector", fn)
                    mu_v = bass.AP(mvt, 0, [[8, 128], [2, 4]])
                    var_v = bass.AP(mvt, 1, [[8, 128], [2, 4]])
                    P.op("vector", lambda v, tb=tb: v.tensor_copy(
                        smu[:, 4 * tb:4 * tb + 4], mu_v))
                    P.edge("vector", lambda v, var_v=var_v: v.tensor_scalar_add(
                        chain[:, 0:4], var_v, EPS))
                    P.edge("vector", lambda v: v.reciprocal(
                        chain[:, 4:8], chain[:, 0:4]))
                    P.edge("vector", lambda v, var_v=var_v: v.tensor_mul(
                        chain[:, 8:12], var_v, chain[:, 4:8]))
                    P.edge("vector", lambda v: v.tensor_scalar_add(
                        chain[:, 8:12], chain[:, 8:12], EPS))
                    P.edge("vector", lambda v: v.reciprocal(
                        chain[:, 12:16], chain[:, 8:12]))
                    P.op("vector", lambda v: v.tensor_mul(
                        chain[:, 0:4], chain[:, 4:8], chain[:, 12:16]), inc=s_ch)
                    P.wait("scalar", s_ch)
                    P.op("scalar", lambda sc, tb=tb: sc.activation(
                        ssc[:, 4 * tb:4 * tb + 4], chain[:, 0:4], ACT.Sqrt),
                        inc=s_sq)

                emit_xloads(0)
                emit_xloads(1)
                emit_ln(0)

                for tb in range(NB):
                    if tb + 2 < NB:
                        emit_xloads(tb + 2)

                    P.wait("vector", s_sq)
                    for j in range(TPB):
                        ti = 4 * tb + j
                        sl = ti % 8
                        if ti >= 2:
                            P.wait("vector", s_tp, ti - 1)
                        P.op("vector", lambda v, ti=ti, sl=sl: v.tensor_scalar(
                            xnb[:, (ti % 2) * 1024:(ti % 2 + 1) * 1024],
                            xin[:, sl * 1024:(sl + 1) * 1024],
                            smu[:, ti:ti + 1], ssc[:, ti:ti + 1],
                            ALU.subtract, ALU.mult), inc=s_app)

                    for j in range(TPB):
                        ti = 4 * tb + j
                        if ti == 0:
                            P.wait("tensor", s_gp, 1)
                        P.wait("tensor", s_app, ti + 1)
                        if ti >= 2:
                            P.wait("tensor", s_xfe, ti - 1)
                        for fc in range(8):
                            P.op("tensor", lambda t, ti=ti, fc=fc: t.transpose(
                                Tp[ti % 2][:, fc * 128:(fc + 1) * 128],
                                xnb[:, (ti % 2) * 1024 + fc * 128:
                                    (ti % 2) * 1024 + (fc + 1) * 128],
                                ident[:]), inc=s_tp if fc == 7 else None)

                    for j in range(TPB):
                        ti = 4 * tb + j
                        P.wait("vector", s_tp, ti + 1)
                        if tb >= 2 and j == 0:
                            P.wait("vector", s_vp, 8 * (tb - 1))
                        dst = bass.AP(xfb, (tb % 2) * 4096 + j * 128,
                                      [[8192, 128], [512, 8], [1, 128]])
                        P.op("vector", lambda v, dst=dst, ti=ti: v.tensor_copy(
                            dst, Tp[ti % 2][:, :]), inc=s_xfe)

                    xof = (tb % 2) * 4096
                    P.wait("tensor", s_xfe, 4 * (tb + 1))
                    if tb == 0:
                        P.wait("tensor", s_wsc)
                    kq = [("k", hp) for hp in range(HP)]
                    if tb < 4:
                        kq += [("q", hp) for hp in range(HP)]
                    if tb == 5:
                        kq += [("qd", hp) for hp in range(HP)]
                    base = s_pp.n
                    pend_nm = []
                    for si, (kind, hp) in enumerate(kq):
                        bank = prj.gsi % 4
                        prj.gsi += 1
                        w = 128 if kind == "qd" else 512
                        xq = xof + (384 if kind == "qd" else 0)
                        if prj.guard[bank] is not None:
                            gsem, gval = prj.guard[bank]
                            P.wait("tensor", gsem, gval)
                        if hp == 0:
                            P.wait("tensor", s_nr)
                        for fc in range(8):
                            if kind == "k":
                                lh = arena1[:, fc * 2048 + hp * 128:
                                            fc * 2048 + (hp + 1) * 128]
                            else:
                                lh = wq_sb[:, fc * 1024 + hp * 128:
                                           fc * 1024 + (hp + 1) * 128]
                            P.op("tensor", lambda t, lh=lh, bank=bank, fc=fc, xq=xq, w=w:
                                 t.matmul(Cp[bank][:, 0:w], lh,
                                          xfb[:, xq + fc * 512: xq + fc * 512 + w],
                                          start=(fc == 0), stop=(fc == 7)),
                                 inc=s_pp if fc == 7 else None)
                        if len(pend_nm) >= 2:
                            pend_nm.pop(0)()
                        P.wait("scalar", s_pp)
                        if kind == "k":
                            if tb * 8 + hp >= 2:
                                P.wait("scalar", s_ksp)
                            dst = kstage[:, (hp % 2) * 512:(hp % 2 + 1) * 512]
                        elif kind == "q":
                            dst = arena2[:, hp * 2048 + tb * 512:
                                         hp * 2048 + tb * 512 + 512]
                        else:
                            dst = qd_sb[:, hp * 128:(hp + 1) * 128]
                        P.op("scalar", lambda sc, dst=dst, bank=bank, w=w: sc.activation(
                            dst, Cp[bank][:, 0:w], ACT.Copy),
                            inc=s_kst if kind == "k" else None)
                        if kind == "k":
                            P.wait("gpsimd", s_kst)
                            P.dma("gpsimd", s_ksp,
                                  ksp_h[tb * 128:(tb + 1) * 128,
                                        hp * 512:(hp + 1) * 512],
                                  kstage[:, (hp % 2) * 512:(hp % 2 + 1) * 512])
                        if si >= 2:
                            P.wait("scalar", s_nm, s_nm.n - 1)
                        P.op("scalar", lambda sc, bank=bank, si=si, w=w: sc.activation(
                            sqb[:, (si % 3) * 512:(si % 3) * 512 + w],
                            Cp[bank][:, 0:w], ACT.Square), inc=s_pe)
                        prj.guard[bank] = (s_pe, s_pe.n)
                        petarget = s_pe.n
                        np_i = 1 if kind == "k" else 0
                        last = (hp == 7)

                        def nm_emit(hp=hp, si=si, np_i=np_i, w=w, petarget=petarget,
                                    last=last, kind=kind, tb=tb):
                            P.wait("tensor", s_pe, petarget)
                            P.op("tensor", lambda t: t.matmul(
                                Np[np_i][:, 0:w], onesblk[:, hp * 16: hp * 16 + 16],
                                sqb[:, (si % 3) * 512:(si % 3) * 512 + w],
                                start=(hp == 0), stop=(hp == 7)), inc=s_nm)
                            if last:
                                P.wait("vector", s_nm)
                                if kind == "k":
                                    P.op("vector", lambda v: v.reduce_max(
                                        normk2[:, tb:tb + 1], Np[1][:, :], AX),
                                        inc=s_nr)
                                elif kind == "q":
                                    P.op("vector", lambda v: v.reduce_max(
                                        normq2[:, tb:tb + 1], Np[0][:, :], AX),
                                        inc=s_nr)
                                else:
                                    P.op("vector", lambda v: v.reduce_max(
                                        normq2d[:, 0:1], Np[0][:, 0:128], AX),
                                        inc=s_nr)
                        pend_nm.append(nm_emit)
                    while pend_nm:
                        pend_nm.pop(0)()

                    if tb + 1 < NB:
                        emit_ln(tb + 1)

                    vbase = s_vp.n
                    vebase = s_ve.n
                    for j in range(TPB):
                        ti = 4 * tb + j
                        for half in range(2):
                            vi = j * 2 + half
                            bank = prj.gsi % 4
                            prj.gsi += 1
                            if prj.guard[bank] is not None:
                                gsem, gval = prj.guard[bank]
                                P.wait("tensor", gsem, gval)
                            for fc in range(8):
                                vincs = s_vp if fc == 7 else None
                                P.op("tensor",
                                     lambda t, fc=fc, j=j, half=half, bank=bank, xof=xof:
                                     t.matmul(Cp[bank][:, :],
                                              xfb[:, xof + fc * 512 + j * 128:
                                                  xof + fc * 512 + (j + 1) * 128],
                                              arena1[:, fc * 2048 + 1024 + half * 512:
                                                     fc * 2048 + 1024 + (half + 1) * 512],
                                              start=(fc == 0), stop=(fc == 7)),
                                     inc=vincs)
                            P.wait("scalar", s_vp)
                            if ti >= 2 and half == 0:
                                P.wait("scalar", s_vs)
                            if ti == 0 and half == 0:
                                P.wait("scalar", s_gp)
                            P.op("scalar", lambda sc, j=j, half=half, bank=bank:
                                 sc.activation(
                                     ap(vstage, (j % 2) * 1040 + half * 8 * 65,
                                        [[65, 8], [1, 64]]),
                                     Cp[bank][:, :], ACT.Copy), inc=s_ve)
                            prj.guard[bank] = (s_ve, s_ve.n)
                        P.wait("gpsimd", s_ve)
                        P.dma("gpsimd", s_vs, vsp_h[ti * 128:(ti + 1) * 128, :],
                              vstage[:, (j % 2) * 1040:(j % 2 + 1) * 1040])


                if debug:
                    sd = P.sem("s_dbg")
                    P.wait("sync", s_pe)
                    P.wait("sync", s_nr)
                    P.dma("sync", sd, dQ_h[:, :], arena2[:, :])
                    P.dma("sync", sd, dst_h[:, 0:NT], smu[:, :])
                    P.dma("sync", sd, dst_h[:, NT:2 * NT], ssc[:, :])
                    P.dma("sync", sd, dnq_h[:, :], normq2[:, :])
                    P.dma("sync", sd, dnk_h[:, :], normk2[:, :])
                    P.dma("sync", sd, dnqd_h[:, :], normq2d[:, :])
                    P.wait("sync", s_vs)
                    P.wait("sync", s_ksp)
                    P.wait("sync", s_x)
                    P.wait("sync", s_x1)
                    P.wait("sync", s_cl)
                    P.wait("sync", s_wl)
                    P.wait("sync", s_role)
                    P.wait("sync", sd)

        if "attn" in phases:
            with ExitStack() as ats:
                EA = lambda *a, **kw: ats.enter_context(nc.sbuf_tensor(*a, **kw))
                EPA = lambda *a, **kw: ats.enter_context(nc.psum_tensor(*a, **kw))
                masks = EA("masks", [128, 896], BF16)
                woa_sb = EA("woa_sb", [128, 8 * 1024], BF16)
                wp1_sb = EA("wp1_sb", [128, 8 * 128], F32)
                vring = EA("vring", [128, 8 * 1040], BF16)
                kring = EA("kring", [128, 2 * 4096], BF16)
                A_sb = EA("A_sb", [128, 4 * 1024], BF16)
                acc = EA("acc", [65, 8 * 512], F32)
                ctxb = EA("ctxb", [128, 8 * 512], BF16)
                ctx_lo = EA("ctx_lo", [64, 8 * 512], BF16)
                xrel = EA("xrel", [128, 2 * 1024], F32)
                t_fm = EA("t_fm", [128, 8], F32)
                u_sb = EA("u_sb", [128, 1], F32)
                lgc = EA("lgc", [8, 1], F32)
                sel_f = EA("sel_f", [1, 8], F32)
                am1 = EA("am1", [1, 8], F32)
                am3 = EA("am3", [1, 2], F32)
                iota8 = EA("iota8", [1, 8], F32)
                mpair = EA("mpair", [1, 8], F32)
                onesel = EA("onesel", [8, 8 * 64], BF16)
                rstat = EA("rstat", [128, 12], F32)
                rmv = EA("rmv", [128, 2], F32)
                rchain = EA("rchain", [128, 8], F32)
                vscr = EA("vscr", [1, 4], F32)
                s_sdr = s("s_sdr")
                Sp = [EPA(f"Sp{i}", [128, 1024], F32) for i in range(2)]
                Xp = [EPA(f"Xp{i}", [65, 1024], F32) for i in range(2)]

                s_gp2 = s("s_gp2")
                s_wo = s("s_wo")
                s_wp = s("s_wp")
                s_attn = s("s_attn")
                bsems = {}
                bsems["s_vt"] = [s("s_vt0"), s("s_vt1")]
                bsems["s_xr"] = [s("s_xr0"), s("s_xr1")]
                for nm in ("s_sc s_ex s_mk s_av s_avp s_acc s_gd s_ls "
                           "s_rc s_rr s_bc s_cm s_cd s_op s_l3 s_r3 "
                           "s_xm s_a3 s_rtx s_rt s_tf s_z1 s_u s_z2 s_lg "
                           "s_lt s_sel s_mz s_mb1 s_mb2 s_mb3 s_mb4 s_mb5 "
                           "s_mb6 s_mb7").split():
                    bsems[nm] = s(nm)

                P.wait("gpsimd", s_vp)
                P.wait("gpsimd", s_ve)
                P.wait("gpsimd", s_nm)
                P.wait("gpsimd", s_vs)
                if debug:
                    P.wait("gpsimd", P.sem("s_dbg"))
                    P.wait("sync", P.sem("s_dbg"))
                P.edge("gpsimd", lambda g: g.memset(masks[:], 1.0))
                P.op("gpsimd", lambda g: g.affine_select(
                    masks[:, :], masks[:, :], [[1, 896]], ALU.is_ge, 0.0,
                    base=-384, channel_multiplier=-1))
                P.edge("gpsimd", lambda g: g.memset(onesel[:, :], 1.0))
                for hd in range(8):
                    P.op("gpsimd", lambda g, hd=hd: g.affine_select(
                        onesel[:, hd * 64:(hd + 1) * 64],
                        onesel[:, hd * 64:(hd + 1) * 64],
                        [[0, 64]], ALU.is_equal, 0.0,
                        base=hd, channel_multiplier=-1))
                P.op("gpsimd", lambda g: g.iota(
                    iota8[:, :], [[1, 8]], channel_multiplier=0,
                    allow_small_or_imprecise_dtypes=True), inc=s_gp2)

                for c in range(8):
                    P.dma("gpsimd", s_wo, woa_sb[0:64, c * 1024:(c + 1) * 1024],
                          woa_h[c * 64:(c + 1) * 64, :])
                    P.dma("gpsimd", s_wo, woa_sb[64:128, c * 1024:(c + 1) * 1024],
                          woa_h[(c + 8) * 64:(c + 9) * 64, :])
                P.wait("sync", s_vp)
                P.wait("sync", s_nm)
                for fc in range(8):
                    P.dma("sync", s_wp, wp1_sb[:, fc * 128:(fc + 1) * 128],
                          wp1_h[fc * 128:(fc + 1) * 128, :])

                C = NS(
                    x_h=x_h, xm_h=xm_h, vsp_h=vsp_h, rtd_h=rtd_h,
                    ident=ident, idf32=idf32, ones_bf=ones_bf, masks=masks,
                    normq2=normq2, normq2d=normq2d, normk2=normk2,
                    kring=kring, ksp_h=ksp_h, arena2=arena2, qd_sb=qd_sb,
                    woa_sb=woa_sb, wp1_sb=wp1_sb, wp2_sb=wp2_sb,
                    bp1c=bp1c, bp2c=bp2c, sel_i=sel_i,
                    lstack=lstack, recip_bf=recip_bf, Mb=Mb, mrow=mrow,
                    mpair=mpair, onesel=onesel, ones_f32=ones_f32,
                    vring=vring, A_sb=A_sb, acc=acc, ctxb=ctxb, ctx_lo=ctx_lo,
                    xrel=xrel,
                    t_fm=t_fm, u_sb=u_sb, lgc=lgc, sel_f=sel_f, am1=am1,
                    am3=am3, iota8=iota8, rstat=rstat, rmv=rmv, rchain=rchain,
                    Sp=Sp, Xp=Xp, s_wo=s_wo, s_wp=s_wp, s_sdr=s_sdr,
                    vscr=vscr, s_attn=s_attn,
                    **bsems)

                gate_list = [(s_vp, None), (s_pe, None), (s_nr, None),
                             (s_ve, None), (s_vs, None), (s_ksp, None),
                             (s_app, None),
                             (s_xfe, None), (s_nm, None), (s_sq, None),
                             (s_wsc, None), (s_role, 16), (s_x, None), (s_x1, None),
                             (s_cl, None), (s_wl, None), (s_gp2, 1)]
                for eng in ENGS:
                    for sm, vv in gate_list:
                        P.wait(eng, sm, vv)

                def br_open(engname):
                    def f(e, engname=engname):
                        r = e.alloc_register("rrole_" + engname)
                        e.reg_load(r, role_sb[0:1, 0:1])
                        gd = e.If_eq(r, 0)
                        P.env[("g", engname)] = gd
                        gd.__enter__()
                    return f

                def br_else(engname):
                    def f(e, engname=engname):
                        P.env[("g", engname)].__exit__(None, None, None)
                        ge = e.Else()
                        P.env[("g", engname)] = ge
                        ge.__enter__()
                    return f

                def br_close(engname):
                    def f(e, engname=engname):
                        P.env[("g", engname)].__exit__(None, None, None)
                    return f

                snap = {nm: sm.n for nm, sm in P.sems.items()}
                for eng in ENGS:
                    P.em(eng, br_open(eng))
                P.branch_tag = "A"
                emit_attention(P, C, 0)
                countsA = {nm: sm.n for nm, sm in P.sems.items()}
                for nm, sm in P.sems.items():
                    sm.n = snap.get(nm, 0)
                for eng in ENGS:
                    P.em(eng, br_else(eng))
                P.branch_tag = "B"
                emit_attention(P, C, 1)
                countsB = {nm: sm.n for nm, sm in P.sems.items()}
                P.branch_tag = ""
                for eng in ENGS:
                    P.em(eng, br_close(eng))
                for nm, sm in P.sems.items():
                    if countsA.get(nm, 0) != countsB.get(nm, 0):
                        sm.n = None
                s_attn.n = 3

                if debug:
                    sd2 = P.sem("s_dbg2")
                    P.wait("sync", s_attn, 3)
                    P.wait("sync", P.sems["s_xm"], 16 * 16)
                    P.dma("sync", sd2, dsel_h[:, :], sel_i[:, :])
                    P.wait("sync", sd2)

        if "ffn" in phases:
            mid.close()
            with ExitStack() as ffs:
                EF = lambda *a, **kw: ffs.enter_context(nc.sbuf_tensor(*a, **kw))
                EPF = lambda *a, **kw: ffs.enter_context(nc.psum_tensor(*a, **kw))
                xmr = EF("xmr", [128, 8 * 1024], F32)
                xn2b = EF("xn2b", [128, 4 * 1024], BF16)
                xf2 = EF("xf2", [128, 2 * 8 * 512], BF16)
                h_sb = EF("h_sb", [128, 16 * 512], BF16)
                r_sb = EF("r_sb", [128, 2 * 8 * 512], BF16)
                wof_sb = EF("wof_sb", [128, 8 * 1024], BF16)
                w2_sb = EF("w2_sb", [128, 16 * 1024], BF16)
                bo3b = EF("bo3b", [128, 1024], F32)
                u1 = EF("u1", [128, 2 * 1024], F32)
                fr = EF("fr", [128, 2 * 1024], F32)
                y_sb = EF("y_sb", [128, 2 * 1024], F32)
                fstat = EF("fstat", [128, 12], F32)
                ftab = EF("ftab", [128, 40], F32)
                bo_row = EF("bo_row", [1, D], F32)
                Tp2 = [EPF(f"Tp2{i}", [128, 1024], BF16) for i in range(2)]
                Hp = [EPF(f"Hp{i}", [128, 512], F32) for i in range(2)]
                Rp = [EPF(f"Rp{i}", [128, 512], F32) for i in range(2)]

                s_g1 = s("s_g1")
                s_g2 = s("s_g2")
                s_wf = s("s_wf")
                s_bor = s("s_bor")
                s_xmL = s("s_xmL")
                s_xmL1 = s("s_xmL1")
                s_fst = s("s_fst")
                s_fsq = s("s_fsq")
                s_fap = s("s_fap")
                s_ftp = s("s_ftp")
                s_fxe = s("s_fxe")
                s_hp = s("s_hp")
                s_hg = s("s_hg")
                s_rp = s("s_rp")
                s_re = s("s_re")
                s_wp = s("s_wp2")
                s_fr = s("s_fr")
                s_f5 = s("s_f5")
                s_q5 = s("s_q5")
                s_yv = s("s_yv")
                s_bo3 = s("s_bo3")
                s_nb = s("s_nb")
                s_u1 = s("s_u1")
                s_n5 = s("s_n5")
                s_ap5 = s("s_ap5")
                s_y = s("s_y")

                P.wait("gpsimd", P.sems["s_sel"], 1)
                P.wait("gpsimd", s_wsc)
                P.wait("gpsimd", s_vp)

                def g_gather(g):
                    r = g.alloc_register("selreg")
                    g.reg_load(r, sel_i[0:1, 0:1])
                    P.env["selv"] = g.snap(r, min_val=0, max_val=7)
                P.em("gpsimd", g_gather)

                def gw1(g):
                    sv = P.env["selv"]
                    src = bass.AP(w1_h, sv * (1024 * FF),
                                  [[FF, 128], [128 * FF, 8], [1, FF]])
                    g.dma_start(arena1[:, :].rearrange("p (f c) -> p f c", f=8),
                                src).then_inc(s_g1.h, 16)
                s_g1.n += 16
                P.em("gpsimd", gw1)

                def gb1(g):
                    sv = P.env["selv"]
                    g.dma_start(b1c[:, :],
                                bass.AP(b1_h, sv * FF, [[1, 128], [128, 16]]),
                                allow_slow_non_contiguous=True
                                ).then_inc(s_g1.h, 16)
                s_g1.n += 16
                P.em("gpsimd", gb1)

                def gb2(g):
                    sv = P.env["selv"]
                    g.dma_start(b2c[:, :],
                                bass.AP(b2_h, sv * D, [[1, 128], [128, 8]]),
                                allow_slow_non_contiguous=True
                                ).then_inc(s_g1.h, 16)
                s_g1.n += 16
                P.em("gpsimd", gb2)
                P.wait("gpsimd", P.sems["s_attn"], 3)
                P.wait("gpsimd", P.sems["s_sdr"], 16)

                P.dma("gpsimd", s_bor, bo_row[:, :],
                      bo_h.rearrange("(o d) -> o d", o=1))

                def gw2(g):
                    sv = P.env["selv"]
                    src = bass.AP(w2_h, sv * (FF * D),
                                  [[D, 128], [128 * D, 16], [1, D]])
                    g.dma_start(w2_sb[:, :].rearrange("p (f c) -> p f c", f=16),
                                src).then_inc(s_g2.h, 16)
                s_g2.n += 16
                P.em("gpsimd", gw2)
                for fc in range(8):
                    P.dma("gpsimd", s_wf, wof_sb[:, fc * 1024:(fc + 1) * 1024],
                          wof_h[fc * 128:(fc + 1) * 128, :])

                for eng in ("sync", "tensor", "vector", "scalar"):
                    P.wait(eng, P.sems["s_attn"], 3)
                P.wait("sync", P.sems["s_xm"], 256)

                P.wait("tensor", s_bor)
                for half in range(2):
                    P.op("tensor", lambda e, half=half: e.matmul(
                        Hp[half][:, :], ones_f32[0:1, :],
                        bo_row[0:1, half * 512:(half + 1) * 512],
                        start=True, stop=True), inc=s_wp)
                P.wait("vector", s_wp)
                for half in range(2):
                    P.op("vector", lambda e, half=half: e.tensor_scalar_mul(
                        bo3b[:, half * 512:(half + 1) * 512],
                        Hp[half][:, :], 0.3), inc=s_bo3)

                def xm_loads(tb2):
                    q = "sync" if tb2 % 2 == 0 else "gpsimd"
                    for j in range(4):
                        ti = 4 * tb2 + j
                        if ti >= 8:
                            P.wait(q, s_yv, ti - 8 + 1)
                        P.dma(q, s_xmL if tb2 % 2 == 0 else s_xmL1,
                              xmr[:, (ti % 8) * 1024:(ti % 8 + 1) * 1024],
                              xm_h[ti * 128:(ti + 1) * 128, :])
                xm_loads(0)
                xm_loads(1)

                def emit_ln2(tb2):
                    scol = 20 if tb2 % 2 == 0 else 28
                    P.wait("vector", s_fsq)
                    P.wait("vector", s_xmL if tb2 % 2 == 0 else s_xmL1,
                           64 * (tb2 // 2 + 1))
                    for j in range(4):
                        ti = 4 * tb2 + j
                        xsl = (ti % 8) * 1024
                        P.op("vector", lambda e, xsl=xsl: e.bn_stats(
                            fstat[:, 0:6], xmr[:, xsl:xsl + 512]))
                        P.edge("vector", lambda e, xsl=xsl: e.bn_stats(
                            fstat[:, 6:12], xmr[:, xsl + 512:xsl + 1024]))
                        P.edge("vector", lambda e, ti=ti: e.bn_aggr(
                            ftab[:, (ti % 8) * 2:(ti % 8) * 2 + 2],
                            fstat[:, 0:12]))
                    fvar_v = bass.AP(ftab, ((4 * tb2) % 8) * 2 + 1,
                                     [[40, 128], [2, 4]])
                    P.edge("vector", lambda e, fvar_v=fvar_v: e.tensor_scalar_add(
                        ftab[:, 16:20], fvar_v, EPS))
                    P.op("vector", lambda e: e.reciprocal(
                        ftab[:, 16:20], ftab[:, 16:20]), inc=s_fst)
                    P.wait("scalar", s_fst)
                    P.op("scalar", lambda e, scol=scol: e.activation(
                        ftab[:, scol:scol + 4], ftab[:, 16:20], ACT.Sqrt),
                        inc=s_fsq)

                def emit_applies(tb2):
                    scol = 20 if tb2 % 2 == 0 else 28
                    P.wait("vector", s_fsq)
                    for j in range(4):
                        ti = 4 * tb2 + j
                        xsl = (ti % 8) * 1024
                        if ti >= 4:
                            P.wait("vector", s_ftp, ti - 3)
                        P.op("vector", lambda e, ti=ti, xsl=xsl, j=j, scol=scol:
                             e.tensor_scalar(
                                 xn2b[:, (ti % 4) * 1024:(ti % 4 + 1) * 1024],
                                 xmr[:, xsl:xsl + 1024],
                                 ftab[:, (ti % 8) * 2:(ti % 8) * 2 + 1],
                                 ftab[:, scol + j:scol + j + 1],
                                 ALU.subtract, ALU.mult), inc=s_fap)

                emit_ln2(0)
                emit_applies(0)
                for tb2 in range(4):
                    for j in range(4):
                        ti = 4 * tb2 + j
                        P.wait("tensor", s_fap, ti + 1)
                        if ti >= 2:
                            P.wait("tensor", s_fxe, ti - 1)
                        for fc in range(8):
                            P.op("tensor", lambda e, ti=ti, fc=fc: e.transpose(
                                Tp2[ti % 2][:, fc * 128:(fc + 1) * 128],
                                xn2b[:, (ti % 4) * 1024 + fc * 128:
                                     (ti % 4) * 1024 + (fc + 1) * 128],
                                ident[:]), inc=s_ftp if fc == 7 else None)
                        P.wait("scalar", s_ftp, ti + 1)
                        if tb2 >= 2 and j == 0:
                            P.wait("scalar", s_hp, 16 * (tb2 - 1))
                        dst = bass.AP(xf2, (tb2 % 2) * 4096 + j * 128,
                                      [[8192, 128], [512, 8], [1, 128]])
                        P.op("scalar", lambda e, dst=dst, ti=ti: e.activation(
                            dst, Tp2[ti % 2][:, :], ACT.Copy), inc=s_fxe)
                    P.wait("tensor", s_fxe, 4 * (tb2 + 1))
                    if s_fr.n >= 2:
                        P.wait("tensor", s_fr, s_fr.n - 2)
                    if tb2 == 0:
                        P.wait("tensor", s_g1)
                        P.wait("tensor", s_bo3)
                    hb = s_hp.n
                    for Fs in range(16):
                        bank = Fs % 2
                        if Fs >= 2:
                            P.wait("tensor", s_hg, s_hg.n - 1)
                        for fc in range(8):
                            P.op("tensor", lambda e, Fs=Fs, fc=fc, bank=bank, tb2=tb2:
                                 e.matmul(Hp[bank][:, :],
                                          arena1[:, fc * 2048 + Fs * 128:
                                                 fc * 2048 + (Fs + 1) * 128],
                                          xf2[:, (tb2 % 2) * 4096 + fc * 512:
                                              (tb2 % 2) * 4096 + (fc + 1) * 512],
                                          start=(fc == 0), stop=(fc == 7)),
                                 inc=s_hp if fc == 7 else None)
                        P.wait("scalar", s_hp)
                        if tb2 >= 1 and Fs < 2:
                            P.wait("scalar", s_rp, 8 * tb2)
                        P.op("scalar", lambda e, Fs=Fs, bank=bank: e.activation(
                            h_sb[:, Fs * 512:(Fs + 1) * 512], Hp[bank][:, :],
                            ACT.Gelu, bias=b1c[:, Fs:Fs + 1], scale=1.0),
                            inc=s_hg)
                    if tb2 == 0:
                        P.wait("tensor", s_g2)
                    P.wait("tensor", s_hg)
                    for os in range(8):
                        bank = os % 2
                        if os >= 2:
                            P.wait("tensor", s_re, s_re.n - 1)
                        for fc2 in range(16):
                            P.op("tensor", lambda e, os=os, fc2=fc2, bank=bank:
                                 e.matmul(Rp[bank][:, :],
                                          w2_sb[:, fc2 * 1024 + os * 128:
                                                fc2 * 1024 + (os + 1) * 128],
                                          h_sb[:, fc2 * 512:(fc2 + 1) * 512],
                                          start=(fc2 == 0), stop=(fc2 == 15)),
                                 inc=s_rp if fc2 == 15 else None)
                        P.wait("scalar", s_rp)
                        P.op("scalar", lambda e, os=os, bank=bank, tb2=tb2:
                             e.activation(
                                 r_sb[:, (tb2 % 2) * 4096 + os * 512:
                                      (tb2 % 2) * 4096 + (os + 1) * 512],
                                 Rp[bank][:, :], ACT.Identity,
                                 bias=b2c[:, os:os + 1], scale=1.0), inc=s_re)
                    if tb2 + 1 < 4:
                        emit_ln2(tb2 + 1)
                        emit_applies(tb2 + 1)
                    if tb2 == 0:
                        P.wait("tensor", s_wf)
                        P.wait("tensor", s_bo3)
                    for j in range(4):
                        ti = 4 * tb2 + j
                        WB = Hp if j % 2 == 0 else Rp
                        P.wait("tensor", s_re)
                        if ti >= 2:
                            P.wait("tensor", s_fr, s_fr.n - 2)
                        for os in range(8):
                            for half in range(2):
                                P.op("tensor", lambda e, os=os, half=half, j=j,
                                     tb2=tb2, WB=WB:
                                     e.matmul(
                                         WB[half][:, :],
                                         r_sb[:, (tb2 % 2) * 4096 + os * 512 +
                                              j * 128:(tb2 % 2) * 4096 +
                                              os * 512 + j * 128 + 128],
                                         wof_sb[:, os * 1024 + half * 512:
                                                os * 1024 + (half + 1) * 512],
                                         start=(os == 0), stop=(os == 7)),
                                     inc=s_wp if (os == 7 and half == 1) else None)
                        xsl = (ti % 8) * 1024
                        rscol = 20 if tb2 % 2 == 0 else 28
                        P.wait("vector", s_wp)
                        P.edge("vector", lambda e, ti=ti, xsl=xsl, j=j,
                               rscol=rscol: e.tensor_scalar(
                            u1[:, (ti % 2) * 1024:(ti % 2 + 1) * 1024],
                            xmr[:, xsl:xsl + 1024],
                            ftab[:, (ti % 8) * 2:(ti % 8) * 2 + 1],
                            ftab[:, rscol + j:rscol + j + 1],
                            ALU.subtract, ALU.mult))
                        P.wait("vector", s_bo3, 2)
                        P.edge("vector", lambda e, ti=ti: e.tensor_add(
                            u1[:, (ti % 2) * 1024:(ti % 2 + 1) * 1024],
                            u1[:, (ti % 2) * 1024:(ti % 2 + 1) * 1024],
                            bo3b[:, :]))
                        for half in range(2):
                            P.op("vector", lambda e, ti=ti, half=half, WB=WB:
                                 e.scalar_tensor_tensor(
                                     fr[:, (ti % 2) * 1024 + half * 512:
                                        (ti % 2) * 1024 + (half + 1) * 512],
                                     WB[half][:, :], 0.3,
                                     u1[:, (ti % 2) * 1024 + half * 512:
                                        (ti % 2) * 1024 + (half + 1) * 512],
                                     ALU.mult, ALU.add), inc=s_fr)
                        P.wait("vector", s_fr)
                        P.edge("vector", lambda e, ti=ti: e.bn_stats(
                            fstat[:, 0:6],
                            fr[:, (ti % 2) * 1024:(ti % 2) * 1024 + 512]))
                        P.edge("vector", lambda e, ti=ti: e.bn_stats(
                            fstat[:, 6:12],
                            fr[:, (ti % 2) * 1024 + 512:(ti % 2) * 1024 + 1024]))
                        P.edge("vector", lambda e: e.bn_aggr(
                            ftab[:, 24:26], fstat[:, 0:12]))
                        P.edge("vector", lambda e: e.tensor_scalar_add(
                            ftab[:, 26:27], ftab[:, 25:26], EPS))
                        P.edge("vector", lambda e: e.tensor_scalar_mul(
                            ftab[:, 32:33], ftab[:, 24:25], -1.0))
                        P.op("vector", lambda e: e.reciprocal(
                            ftab[:, 27:28], ftab[:, 26:27]), inc=s_f5)
                        P.wait("scalar", s_f5)
                        P.op("scalar", lambda e: e.activation(
                            ftab[:, 26:27], ftab[:, 27:28], ACT.Sqrt), inc=s_q5)
                        P.op("scalar", lambda e, ti=ti: e.activation(
                            y_sb[:, (ti % 2) * 1024:(ti % 2 + 1) * 1024],
                            fr[:, (ti % 2) * 1024:(ti % 2 + 1) * 1024],
                            ACT.Identity, bias=ftab[:, 32:33], scale=1.0),
                            inc=s_ap5)
                        P.wait("vector", s_ap5)
                        P.wait("vector", s_q5)
                        P.wait("vector", s_y)
                        P.op("vector", lambda e, ti=ti, xsl=xsl:
                             e.scalar_tensor_tensor(
                                 y_sb[:, (ti % 2) * 1024:(ti % 2 + 1) * 1024],
                                 y_sb[:, (ti % 2) * 1024:(ti % 2 + 1) * 1024],
                                 ftab[:, 26:27],
                                 xmr[:, xsl:xsl + 1024],
                                 ALU.mult, ALU.add), inc=s_yv)
                        P.wait("sync", s_yv)
                        P.dma("sync", s_y, y_h[ti * 128:(ti + 1) * 128, :],
                              y_sb[:, (ti % 2) * 1024:(ti % 2 + 1) * 1024])
                    if tb2 + 2 <= 3:
                        xm_loads(tb2 + 2)

                P.wait("sync", s_y)
                P.wait("sync", s_xmL)
                P.wait("sync", s_xmL1)
                P.wait("sync", s_g1)
                P.wait("sync", s_g2)
                P.wait("sync", s_wf)
                P.wait("sync", s_bor)
                P.wait("sync", s_role, 16)
                P.wait("sync", s_cl)
                P.wait("sync", s_wl)
                P.wait("sync", s_x)
                P.wait("sync", s_x1)
                P.wait("sync", s_vs)
                P.wait("sync", s_ksp)
                P.wait("sync", P.sems["s_sdr"], 16)

        with nc.Block() as block:
            block.sync(lambda e: [f(e) for f in P.ops["sync"]])
            block.tensor(lambda e: [f(e) for f in P.ops["tensor"]])
            block.vector(lambda e: [f(e) for f in P.ops["vector"]])
            block.scalar(lambda e: [f(e) for f in P.ops["scalar"]])
            block.gpsimd(lambda e: [f(e) for f in P.ops["gpsimd"]])

    return nc



def _perm_for(role):
    return BLOCKS[role] + BLOCKS[1 - role]


def _in_maps(x, args):
    (g_ln, Wq, Wkv, Woa, Wp1, bp1, Wp2, bp2, W1, b1, W2, b2, Wof, bo) = args
    in_maps = []
    for c in range(8):
        b, role = c // 2, c % 2
        perm = _perm_for(role)
        xp = np.concatenate([x[b, blk * 512:(blk + 1) * 512] for blk in perm], axis=0)
        in_maps.append({
            "x": np.ascontiguousarray(xp),
            "role": np.array([[role]], dtype=np.int32),
            "g_ln": np.asarray(g_ln, np.float32),
            "Wq": np.asarray(Wq, np.float32),
            "Wkv": np.asarray(Wkv, np.float32),
            "Wo_attn": np.asarray(Woa, np.float32),
            "Wp1": np.asarray(Wp1, np.float32),
            "bp1": np.asarray(bp1, np.float32),
            "Wp2": np.asarray(Wp2, np.float32),
            "bp2": np.asarray(bp2, np.float32),
            "W1": np.ascontiguousarray(np.asarray(W1, np.float32).reshape(E * D, FF)),
            "b1": np.asarray(b1, np.float32),
            "W2": np.ascontiguousarray(np.asarray(W2, np.float32).reshape(E * FF, D)),
            "b2": np.asarray(b2, np.float32),
            "Wo_ffn": np.asarray(Wof, np.float32),
            "bo": np.asarray(bo, np.float32),
        })
    return in_maps


def kernel(L0_attn_x, L0_attn_g_ln, L0_attn_Wq, L0_attn_Wkv, L0_attn_Wo,
           L0_ffn_Wp1, L0_ffn_bp1, L0_ffn_Wp2, L0_ffn_bp2,
           L0_ffn_W1, L0_ffn_b1, L0_ffn_W2, L0_ffn_b2, L0_ffn_Wo, L0_ffn_bo):
    x = np.ascontiguousarray(np.asarray(L0_attn_x, dtype=np.float32))
    nc = build()
    in_maps = _in_maps(x, (L0_attn_g_ln, L0_attn_Wq, L0_attn_Wkv, L0_attn_Wo,
                           L0_ffn_Wp1, L0_ffn_bp1, L0_ffn_Wp2, L0_ffn_bp2,
                           L0_ffn_W1, L0_ffn_b1, L0_ffn_W2, L0_ffn_b2,
                           L0_ffn_Wo, L0_ffn_bo))
    res = run_bass_kernel_spmd(nc, in_maps, core_ids=list(range(8)))
    y = np.zeros((B, S, D), dtype=np.float32)
    for c in range(8):
        b, role = c // 2, c % 2
        yc = res.results[c]["y"]
        for si, blk in enumerate(BLOCKS[role]):
            y[b, blk * 512:(blk + 1) * 512] = yc[si * 512:(si + 1) * 512]
    return y
